# version 25
# baseline (speedup 1.0000x reference)
import numpy as np
import ml_dtypes
from contextlib import ExitStack
import concourse.bass as bass
import concourse.mybir as mybir
from concourse.bass_utils import run_bass_kernel_spmd

F32 = mybir.dt.float32
BF16 = mybir.dt.bfloat16
I8 = mybir.dt.int8
AF = mybir.ActivationFunctionType
ALU = mybir.AluOpType
AX = mybir.AxisListType
bf16 = ml_dtypes.bfloat16

T = 2048
D = 2048
INTER = 8192
STRIPE = 512
NS = T // STRIPE
NTT = STRIPE // 128
C = 64
NCH = STRIPE // C
DK = 128
EPS = 1e-5
NHC = D // 128


class Sem:
    def __init__(self, nc, ctx, name):
        self.h = ctx.enter_context(nc.semaphore(name))
        self.n = 0


class Prog:
    ENGINES = ("sync", "scalar", "vector", "tensor", "gpsimd")

    def __init__(self):
        self.q = {e: [] for e in self.ENGINES}

    def op(self, eng, name, *args, inc=None, inck=1, **kw):
        if inc is not None:
            inc.n += inck
        self.q[eng].append(("op", name, args, kw, (inc.h if inc else None), inck))
        return inc.n if inc else None

    def wait(self, eng, sem, val):
        self.q[eng].append(("wait", sem.h, val))

    def drain(self, eng):
        self.q[eng].append(("drain",))

    def replay(self, eng_obj, name):
        for item in self.q[name]:
            if item[0] == "wait":
                eng_obj.wait_ge(item[1], item[2])
            elif item[0] == "drain":
                eng_obj.drain()
            else:
                _, opname, args, kw, semh, inck = item
                ins = getattr(eng_obj, opname)(*args, **kw)
                if semh is not None:
                    ins.then_inc(semh, inck)


def build_program(phases=(1, 2, 3)):
    nc = bass.Bass("TRN2", target_bir_lowering=False)

    x0 = nc.dram_tensor("x0", [T, D], F32, kind="ExternalInput").ap()
    wq = nc.dram_tensor("wq", [DK, D], F32, kind="ExternalInput").ap()
    wk = nc.dram_tensor("wk", [DK, D], F32, kind="ExternalInput").ap()
    wog = nc.dram_tensor("wog", [D, D], F32, kind="ExternalInput").ap()
    wig = nc.dram_tensor("wig", [D, D], F32, kind="ExternalInput").ap()
    wo = nc.dram_tensor("wo", [D, D], F32, kind="ExternalInput").ap()
    gnw = nc.dram_tensor("gnw", [D], F32, kind="ExternalInput").ap()
    wg8 = nc.dram_tensor("wg8b", [128, (INTER // 256) * NHC * 256], I8, kind="ExternalInput").ap()
    wu8 = nc.dram_tensor("wu8b", [128, (INTER // 256) * NHC * 256], I8, kind="ExternalInput").ap()
    wd8 = nc.dram_tensor("wd8b", [128, (INTER // 256) * 2 * D], I8, kind="ExternalInput").ap()
    sg = nc.dram_tensor("sg", [INTER], F32, kind="ExternalInput").ap()
    su = nc.dram_tensor("su", [INTER], F32, kind="ExternalInput").ap()
    sd = nc.dram_tensor("sd", [D], F32, kind="ExternalInput").ap()
    ci64b = nc.dram_tensor("ci64b", [64, 64], BF16, kind="ExternalInput").ap()
    ci128b = nc.dram_tensor("ci128b", [128, 128], BF16, kind="ExternalInput").ap()
    ci64f = nc.dram_tensor("ci64f", [64, 64], F32, kind="ExternalInput").ap()
    ctriu = nc.dram_tensor("ctriu", [64, 64], F32, kind="ExternalInput").ap()
    cones = nc.dram_tensor("cones", [128, 128], BF16, kind="ExternalInput").ap()
    cmisc = nc.dram_tensor("cmisc", [128, 2], F32, kind="ExternalInput").ap()
    out = nc.dram_tensor("out", [T, D], F32, kind="ExternalOutput").ap()
    x1s = nc.dram_tensor("x1s", [T, D], F32, kind="Internal").ap()
    aos = nc.dram_tensor("aos", [T, D], F32, kind="Internal").ap()
    wogb = nc.dram_tensor("wogb", [D, D], BF16, kind="Internal").ap()
    wigb = nc.dram_tensor("wigb", [D, D], BF16, kind="Internal").ap()
    wob = nc.dram_tensor("wob", [D, D], BF16, kind="Internal").ap()
    wogT = nc.dram_tensor("wogT", [128, NHC * D], BF16, kind="Internal").ap()
    wigT = nc.dram_tensor("wigT", [128, NHC * D], BF16, kind="Internal").ap()
    woT = nc.dram_tensor("woT", [D, D], BF16, kind="Internal").ap()

    P = Prog()
    ctx = ExitStack()
    with ctx:
        E = ctx.enter_context
        sPE = Sem(nc, ctx, "sPE")
        sACT = Sem(nc, ctx, "sACT")
        sDVE = Sem(nc, ctx, "sDVE")
        sBAR = Sem(nc, ctx, "sBAR")
        sPL = Sem(nc, ctx, "sPL")
        dC = Sem(nc, ctx, "dC")
        dP = Sem(nc, ctx, "dP")
        dQ = Sem(nc, ctx, "dQ")
        dT = Sem(nc, ctx, "dT")
        SL = {k: Sem(nc, ctx, "d_" + k) for k in
              ("wg0", "wg1", "wu0", "wu1", "wd0", "wd1",
               "xa0", "xa1", "xb0", "xb1", "awg0", "awg1", "awu0", "awu1",
               "awg2", "awg3", "awu2", "awu3",
               "xat0", "xat1", "wo0", "wo1", "ao0", "ao1")}

        t_i64b = E(nc.sbuf_tensor("t_i64b", [64, 64], BF16))
        t_i128b = E(nc.sbuf_tensor("t_i128b", [128, 128], BF16))
        t_i64f = E(nc.sbuf_tensor("t_i64f", [64, 64], F32))
        t_triu = E(nc.sbuf_tensor("t_triu", [64, 64], F32))
        t_ones = E(nc.sbuf_tensor("t_ones", [128, 128], BF16))
        t_sgp = E(nc.sbuf_tensor("t_sgp", [128, 64], F32))
        t_sup = E(nc.sbuf_tensor("t_sup", [128, 64], F32))
        t_gnw = E(nc.sbuf_tensor("t_gnw", [128, 16], F32))
        t_sdb = E(nc.sbuf_tensor("t_sdb", [128, D], F32))
        t_wqT = E(nc.sbuf_tensor("t_wqT", [128, NHC * DK], BF16))
        t_wkT = E(nc.sbuf_tensor("t_wkT", [128, NHC * DK], BF16))
        t_S = E(nc.sbuf_tensor("t_S", [128, D], F32))
        t_Sb = E(nc.sbuf_tensor("t_Sb", [128, D], BF16))
        t_yT = E(nc.sbuf_tensor("t_yT", [128, NHC * STRIPE], BF16))
        t_wg = [E(nc.sbuf_tensor(f"t_wg{i}", [128, NHC * 256], BF16)) for i in range(2)]
        t_wu = [E(nc.sbuf_tensor(f"t_wu{i}", [128, NHC * 256], BF16)) for i in range(2)]
        t_ptr = E(nc.sbuf_tensor("t_ptr", [128, 512], BF16))
        t_stat = E(nc.sbuf_tensor("t_stat", [128, 16], F32))
        t_cm = E(nc.sbuf_tensor("t_cm", [128, 2], F32))

        pA = [E(nc.psum_tensor(f"pA{i}", [128, 512], F32)) for i in range(4)]
        pB = [E(nc.psum_tensor(f"pB{i}", [128, 512], F32)) for i in range(2)]
        pF = E(nc.psum_tensor("pF", [128, 512], F32))
        pT = E(nc.psum_tensor("pT", [128, 1024], BF16))

        pctx = ExitStack()
        t_sdr = pctx.enter_context(nc.sbuf_tensor("t_sdr", [1, D], BF16))
        t_wqk = pctx.enter_context(nc.sbuf_tensor("t_wqk", [128, D], BF16))
        P.op("scalar", "dma_start", out=t_i64b[:], in_=ci64b[:], inc=dC, inck=16)
        P.op("scalar", "dma_start", out=t_i128b[:], in_=ci128b[:], inc=dC, inck=16)
        P.op("scalar", "dma_start", out=t_i64f[:], in_=ci64f[:], inc=dC, inck=16)
        P.op("scalar", "dma_start", out=t_triu[:], in_=ctriu[:], inc=dC, inck=16)
        P.op("scalar", "dma_start", out=t_ones[:], in_=cones[:], inc=dC, inck=16)
        P.op("scalar", "dma_start", out=t_sgp[:], allow_slow_non_contiguous=True,
             in_=sg.rearrange("(c p) -> p c", p=128), inc=dC, inck=16)
        P.op("scalar", "dma_start", out=t_sup[:], allow_slow_non_contiguous=True,
             in_=su.rearrange("(c p) -> p c", p=128), inc=dC, inck=16)
        P.op("scalar", "dma_start", out=t_gnw[:], allow_slow_non_contiguous=True,
             in_=gnw.rearrange("(c p) -> p c", p=128), inc=dC, inck=16)
        P.op("scalar", "dma_start", out=t_cm[:], in_=cmisc[:], inc=dC, inck=16)
        CONSTS = dC.n

        P.op("gpsimd", "dma_start", out=t_sdr[:], in_=sd.rearrange("(one d) -> one d", one=1),
             inc=dP, inck=16)
        P.op("gpsimd", "dma_start", out=t_wqk[:], in_=wq[:], inc=dP, inck=16)
        P.wait("tensor", dC, CONSTS)
        P.wait("tensor", dP, 32)

        ptfree = [0]

        def transpose_128(src_tile, dst_colf, n_chunks, ident, rows):
            P.wait("tensor", sDVE, ptfree[0])
            for hc in range(n_chunks):
                slotp = hc % 8
                P.op("tensor", "transpose", pT[:, slotp * 128:slotp * 128 + rows],
                     src_tile[:, hc * 128:(hc + 1) * 128], ident, inc=sPE)
                if slotp == 7 or hc == n_chunks - 1:
                    nsl = slotp + 1
                    P.wait("vector", sPE, sPE.n)
                    for h2 in range(nsl):
                        hcc = hc - nsl + 1 + h2
                        P.op("vector", "tensor_copy", dst_colf(hcc),
                             pT[:, h2 * 128:h2 * 128 + rows], inc=sDVE)
                    ptfree[0] = sDVE.n
                    P.wait("tensor", sDVE, sDVE.n)

        transpose_128(t_wqk, lambda hc: t_wqT[:, hc * DK:(hc + 1) * DK], NHC, t_i128b[:], 128)
        P.wait("gpsimd", sDVE, sDVE.n)
        P.op("gpsimd", "dma_start", out=t_wqk[:], in_=wk[:], inc=dP, inck=16)
        P.wait("tensor", dP, 48)
        transpose_128(t_wqk, lambda hc: t_wkT[:, hc * DK:(hc + 1) * DK], NHC, t_i128b[:], 128)

        for dc2 in range(4):
            P.op("tensor", "matmul", pF[:, 0:512], t_ones[0:1, :], t_sdr[:, dc2 * 512:(dc2 + 1) * 512],
                 start=True, stop=True, inc=sPE)
            P.wait("vector", sPE, sPE.n)
            P.op("vector", "tensor_scalar_mul", t_sdb[:, dc2 * 512:(dc2 + 1) * 512], pF[:, 0:512], 0.5,
                 inc=sDVE)
            P.wait("tensor", sDVE, sDVE.n)

        PROLOGUE_DVE = sDVE.n
        pctx.close()
        P.op("gpsimd", "dma_start", out=wogb[:], in_=wog[:], inc=dQ, inck=16)
        P.op("gpsimd", "dma_start", out=wigb[:], in_=wig[:], inc=dQ, inck=16)
        P.op("gpsimd", "dma_start", out=wob[:], in_=wo[:], inc=dQ, inck=16)
        CASTS = dQ.n
        P.wait("sync", dQ, CASTS)
        for srcw, dst, blocked in ((wogb, wogT, True), (wigb, wigT, True), (wob, woT, False)):
            for hc in range(NHC):
                for m4 in range(4):
                    P.op("sync", "dma_start_transpose", t_ptr[:],
                         srcw[m4 * 512:(m4 + 1) * 512, hc * 128:(hc + 1) * 128], inc=dT, inck=16)
                    P.wait("sync", dT, dT.n)
                    if blocked:
                        P.op("sync", "dma_start",
                             out=dst[:].rearrange("p (mc hcb m) -> p mc hcb m", mc=NHC, hcb=NHC)
                                 [:, m4 * 4:(m4 + 1) * 4, hc, :],
                             in_=t_ptr[:].rearrange("p (mcl m) -> p mcl m", mcl=4),
                             inc=dT, inck=16)
                    else:
                        P.op("sync", "dma_start",
                             out=dst[hc * 128:(hc + 1) * 128, m4 * 512:(m4 + 1) * 512],
                             in_=t_ptr[:], inc=dT, inck=16)
                    P.wait("sync", dT, dT.n)
        PREP = dT.n

        def emit_ln(xt, yt, sqt, statt, st, rows, ve="vector", sVE=None):
            if sVE is None:
                sVE = sDVE
            P.op("vector", "reduce_sum", statt[:, st:st + 1], xt, AX.X, inc=sDVE)
            P.op("scalar", "activation", sqt, xt, AF.Square,
                 accum_out=statt[:, st + 1:st + 2], inc=sACT)
            P.wait(ve, sACT, sACT.n)
            if ve != "vector":
                P.wait(ve, sDVE, sDVE.n)
            P.drain(ve)
            P.op(ve, "tensor_scalar_mul", statt[:, st + 2:st + 3], statt[:, st:st + 1], 1.0 / D)
            P.drain(ve)
            P.op(ve, "tensor_scalar", statt[:, st + 3:st + 4], statt[:, st + 2:st + 3],
                 statt[:, st + 2:st + 3], -1.0, ALU.mult, ALU.mult)
            P.drain(ve)
            P.op(ve, "scalar_tensor_tensor", statt[:, st + 4:st + 5], statt[:, st + 1:st + 2],
                 1.0 / D, statt[:, st + 3:st + 4], ALU.mult, ALU.add, inc=sVE)
            P.wait("scalar", sVE, sVE.n)
            P.op("scalar", "activation", statt[:, st + 5:st + 6], statt[:, st + 4:st + 5], AF.Ln,
                 bias=t_cm[0:rows, 0:1])
            P.drain("scalar")
            P.op("scalar", "activation", statt[:, st + 6:st + 7], statt[:, st + 5:st + 6], AF.Exp,
                 scale=-0.5, inc=sACT)
            P.wait(ve, sACT, sACT.n)
            P.op(ve, "tensor_scalar", statt[:, st + 7:st + 8], statt[:, st + 2:st + 3],
                 statt[:, st + 6:st + 7], -1.0, ALU.mult, ALU.mult, inc=sVE)
            P.wait("scalar", sVE, sVE.n)
            P.drain("scalar")
            P.op("scalar", "activation", yt, xt, AF.Identity,
                 bias=statt[:, st + 7:st + 8], scale=statt[:, st + 6:st + 7], inc=sACT)
            return sACT.n

        def ffn_phase(phase):
            xin = x0 if phase == 1 else x1s
            xadd = None if phase == 1 else aos
            xout = x1s if phase == 1 else out
            with ExitStack() as rctx:
                RE = rctx.enter_context
                t_xa = [RE(nc.sbuf_tensor(f"f{phase}_xa{i}", [128, D], F32)) for i in range(4)]
                t_ab = [RE(nc.sbuf_tensor(f"f{phase}_ab0", [128, D], F32))]
                t_fuse = RE(nc.sbuf_tensor(f"f{phase}_fuse", [128, 64 * STRIPE], BF16))
                t_g = [RE(nc.sbuf_tensor(f"f{phase}_g{i}", [128, 512], BF16)) for i in range(2)]

                t_sq = RE(nc.sbuf_tensor(f"f{phase}_sq", [128, D], BF16))
                t_wd = [RE(nc.sbuf_tensor(f"f{phase}_wd{i}", [128, 4 * 512], BF16)) for i in range(2)]

                xa_free = [(sDVE, PROLOGUE_DVE)] * 4 if phase == 1 else [None] * 4
                ab_free = [None]
                g_free = [None, None]
                wbusy = {}
                fuse_ready = {}
                lnyT_ready = {}
                y_sq_free = [0]
                XS = ["xa0", "xa1", "xb0", "xb1"]
                ABS = ["xat0", "xat1"]

                def issue_gu_block(ib):
                    wslot = ib % 2
                    for nm, w8t in (("g", wg8), ("u", wu8)):
                        buf = (t_wg if nm == "g" else t_wu)[wslot]
                        key = (nm, wslot)
                        if key in wbusy:
                            P.wait("gpsimd", wbusy[key][0], wbusy[key][1])
                        sl = SL[("wg" if nm == "g" else "wu") + str(wslot)]
                        P.op("gpsimd", "dma_start",
                             out=buf[:],
                             in_=w8t[:, ib * 4096:(ib + 1) * 4096],
                             inc=sl, inck=16)

                def issue_wd_tile(dc, icg):
                    wdslot = icg % 2
                    key = ("d", wdslot)
                    if key in wbusy:
                        P.wait("gpsimd", wbusy[key][0], wbusy[key][1])
                    P.op("gpsimd", "dma_start",
                         out=t_wd[wdslot][:],
                         in_=wd8[:, (dc * 16 + icg) * 2048:(dc * 16 + icg) * 2048 + 2048],
                         inc=SL[f"wd{wdslot}"], inck=16)

                for s in range(NS):
                    for tt in range(NTT):
                        gt = s * NTT + tt
                        sx = SL[XS[tt]]
                        if xa_free[tt] is not None:
                            P.wait("scalar", xa_free[tt][0], xa_free[tt][1])
                        P.op("scalar", "dma_start", out=t_xa[tt][:],
                             in_=xin[gt * 128:(gt + 1) * 128, :], inc=sx, inck=16)
                        if xadd is not None:
                            sb = SL[ABS[0]]
                            if ab_free[0] is not None:
                                P.wait("scalar", ab_free[0][0], ab_free[0][1])
                            P.op("scalar", "dma_start", out=t_ab[0][:],
                                 in_=xadd[gt * 128:(gt + 1) * 128, :], inc=sb, inck=16)
                            P.wait("vector", sx, sx.n)
                            P.wait("vector", sb, sb.n)
                            P.drain("vector")
                            P.op("vector", "tensor_add", t_xa[tt][:], t_xa[tt][:], t_ab[0][:],
                                 inc=sDVE)
                            ab_free[0] = (sDVE, sDVE.n)
                            rdy = (sDVE, sDVE.n)
                        else:
                            rdy = (sx, sx.n)
                        st = (tt % 2) * 8
                        P.wait("vector", rdy[0], rdy[1])
                        P.wait("scalar", rdy[0], rdy[1])
                        if y_sq_free[0]:
                            P.wait("scalar", sPE, y_sq_free[0])
                        emit_ln(t_xa[tt][:], t_sq[:], t_sq[:], t_stat, st, 128)
                        y_done = sACT.n
                        P.wait("tensor", sACT, y_done)
                        transpose_128(
                            t_sq,
                            lambda hcc, tt=tt: t_yT[:, hcc * STRIPE + tt * 128: hcc * STRIPE + tt * 128 + 128],
                            NHC, t_i128b[:], 128)
                        y_sq_free[0] = sPE.n
                    lnyT_ready[s] = sDVE.n

                    issue_gu_block(0)
                    for ib in range(INTER // 256):
                        wslot = ib % 2
                        if ib + 1 < INTER // 256:
                            issue_gu_block(ib + 1)
                        for icc in range(2):
                            ic = ib * 2 + icc
                            pg = pA[0 + 2 * (ic % 2)]
                            pu = pA[1 + 2 * (ic % 2)]
                            P.wait("tensor", SL[f"wg{wslot}"], SL[f"wg{wslot}"].n)
                            P.wait("tensor", SL[f"wu{wslot}"], SL[f"wu{wslot}"].n)
                            if ic < 2:
                                P.wait("tensor", sDVE, lnyT_ready[s])
                            else:
                                if ic < 4:
                                    P.wait("tensor", sDVE, lnyT_ready[s])
                                P.wait("tensor", sDVE, fuse_ready[(s, ic - 2)])
                            for hc in range(NHC):
                                P.op("tensor", "matmul", pg[:, 0:512],
                                     t_wg[wslot][:, hc * 256 + icc * 128: hc * 256 + icc * 128 + 128],
                                     t_yT[:, hc * STRIPE:(hc + 1) * STRIPE],
                                     start=(hc == 0), stop=(hc == NHC - 1),
                                     inc=(sPE if hc == NHC - 1 else None))
                                P.op("tensor", "matmul", pu[:, 0:512],
                                     t_wu[wslot][:, hc * 256 + icc * 128: hc * 256 + icc * 128 + 128],
                                     t_yT[:, hc * STRIPE:(hc + 1) * STRIPE],
                                     start=(hc == 0), stop=(hc == NHC - 1),
                                     inc=(sPE if hc == NHC - 1 else None))
                            u_mm = sPE.n
                            g_mm = u_mm - 1
                            if icc == 1:
                                wbusy[("g", wslot)] = (sPE, u_mm)
                                wbusy[("u", wslot)] = (sPE, u_mm)
                            eslot = ic % 2
                            P.wait("scalar", sPE, g_mm)
                            if g_free[eslot] is not None:
                                P.wait("scalar", sDVE, g_free[eslot])
                            P.op("scalar", "activation", t_g[eslot][:], pg[:, 0:512], AF.Silu,
                                 scale=t_sgp[:, ic:ic + 1], inc=sACT)
                            P.wait("vector", sACT, sACT.n)
                            P.wait("vector", sPE, u_mm)
                            P.drain("vector")
                            P.op("vector", "scalar_tensor_tensor", t_fuse[:, ic * 512:(ic + 1) * 512],
                                 pu[:, 0:512], t_sup[:, ic:ic + 1], t_g[eslot][:],
                                 ALU.mult, ALU.mult, inc=sDVE)
                            fuse_ready[(s, ic)] = sDVE.n
                            g_free[eslot] = sDVE.n

                    dpsum = [pB[0], pB[1], pA[0], pA[1]]
                    ev_done = {}
                    issue_wd_tile(0, 0)
                    for dc in range(4):
                        for icg in range(16):
                            wdslot = icg % 2
                            if icg + 1 < 16:
                                issue_wd_tile(dc, icg + 1)
                            elif dc + 1 < 4:
                                issue_wd_tile(dc + 1, 0)
                            P.wait("tensor", SL[f"wd{wdslot}"], SL[f"wd{wdslot}"].n)
                            if icg == 0:
                                P.wait("tensor", sDVE, fuse_ready[(s, 63)])
                                for tt in range(NTT):
                                    if (dc - 1, tt) in ev_done:
                                        P.wait("tensor", sDVE, ev_done[(dc - 1, tt)])
                            for j in range(4):
                                ic = icg * 4 + j
                                for tt in range(NTT):
                                    P.op("tensor", "matmul", dpsum[tt][:, 0:512],
                                         t_fuse[:, ic * 512 + tt * 128: ic * 512 + tt * 128 + 128],
                                         t_wd[wdslot][:, j * 512:(j + 1) * 512],
                                         start=(ic == 0), stop=(ic == 63),
                                         inc=(sPE if (tt == NTT - 1 and j == 3) else None))
                            wbusy[("d", wdslot)] = (sPE, sPE.n)
                        down_mm = sPE.n
                        for tt in range(NTT):
                            P.wait("vector", sPE, down_mm)
                            dslot = tt % 2
                            t_dt = t_ab[0][:, dslot * 512:dslot * 512 + 512]
                            P.drain("vector")
                            P.op("vector", "tensor_mul", t_dt,
                                 dpsum[tt][:, 0:512], t_sdb[:, dc * 512:(dc + 1) * 512])
                            P.drain("vector")
                            P.op("vector", "tensor_add", t_xa[tt][:, dc * 512:(dc + 1) * 512],
                                 t_xa[tt][:, dc * 512:(dc + 1) * 512], t_dt,
                                 inc=sDVE)
                            ev_done[(dc, tt)] = sDVE.n
                    ab_free[0] = (sDVE, sDVE.n)
                    for tt in range(NTT):
                        gt = s * NTT + tt
                        P.wait("scalar", sDVE, ev_done[(3, tt)])
                        sx = SL[XS[tt]]
                        P.op("scalar", "dma_start", out=xout[gt * 128:(gt + 1) * 128, :],
                             in_=t_xa[tt][:], inc=sx, inck=16)
                        xa_free[tt] = (sx, sx.n)

        def barrier():
            P.wait("gpsimd", dQ, CASTS)
            for k in ("wg0", "wg1", "wu0", "wu1", "wd0", "wd1"):
                P.wait("gpsimd", SL[k], SL[k].n)
            for k in ("xa0", "xa1", "xb0", "xb1", "xat0", "xat1", "ao0", "ao1"):
                P.wait("scalar", SL[k], SL[k].n)
            P.wait("sync", dT, dT.n)
            for k in ("wo0", "wo1", "awg0", "awg1", "awu0", "awu1",
                      "awg2", "awg3", "awu2", "awu3"):
                P.wait("sync", SL[k], SL[k].n)
            target = sBAR.n + 5
            for e in Prog.ENGINES:
                if e in ("scalar", "vector", "tensor", "gpsimd"):
                    P.drain(e)
                P.op(e, "nop", inc=sBAR)
            for e in Prog.ENGINES:
                P.wait(e, sBAR, target)

        def attn_phase():
            with ExitStack() as rctx:
                RE = rctx.enter_context
                t_xat = [RE(nc.sbuf_tensor(f"a_x{i}", [64, D], F32)) for i in range(2)]
                t_vh = [RE(nc.sbuf_tensor(f"a_vh{i}", [64, D], BF16)) for i in range(8)]
                t_sq6 = RE(nc.sbuf_tensor("a_sq6", [64, D], BF16))
                t_oT = RE(nc.sbuf_tensor("a_oT", [128, NHC * STRIPE], BF16))
                t_z = RE(nc.sbuf_tensor("a_z", [128, NHC * STRIPE], BF16))

                t_sgt = [RE(nc.sbuf_tensor(f"a_sg{i}", [128, 512], BF16)) for i in range(2)]
                t_aoe = [RE(nc.sbuf_tensor(f"a_ao{i}", [128, 512], F32)) for i in range(2)]
                t_zt = t_aoe[0]
                t_wot = [RE(nc.sbuf_tensor(f"a_wo{i}", [128, 512], BF16)) for i in range(2)]
                t_qz = [RE(nc.sbuf_tensor(f"a_qz{i}", [64, 128], F32)) for i in range(2)]
                t_qb = [RE(nc.sbuf_tensor(f"a_qb{i}", [64, 128], BF16)) for i in range(2)]
                t_kf = [RE(nc.sbuf_tensor(f"a_kf{i}", [64, 128], F32)) for i in range(2)]
                t_kb = [RE(nc.sbuf_tensor(f"a_kb{i}", [64, 128], BF16)) for i in range(2)]
                t_e1 = [RE(nc.sbuf_tensor(f"a_e1{i}", [64, 128], F32)) for i in range(2)]
                t_sp = [RE(nc.sbuf_tensor(f"a_sp{i}", [64, 128], F32)) for i in range(2)]
                t_nb = [RE(nc.sbuf_tensor(f"a_nb{i}", [64, 128], F32)) for i in range(2)]
                t_kzf = [RE(nc.sbuf_tensor(f"a_kzf{i}", [64, 128], F32)) for i in range(2)]
                t_nbT = [RE(nc.sbuf_tensor(f"a_nbT{i}", [128, 64], F32)) for i in range(2)]
                t_eB = [RE(nc.sbuf_tensor(f"a_eB{i}", [128, 64], F32)) for i in range(2)]
                t_emB = [RE(nc.sbuf_tensor(f"a_emB{i}", [128, 64], F32)) for i in range(2)]
                t_ed = [RE(nc.sbuf_tensor(f"a_ed{i}", [128, 64], F32)) for i in range(2)]
                t_eld = [RE(nc.sbuf_tensor(f"a_eld{i}", [128, 1], F32)) for i in range(8)]
                t_qT = [RE(nc.sbuf_tensor(f"a_qT{i}", [128, 64], BF16)) for i in range(2)]
                t_kT = [RE(nc.sbuf_tensor(f"a_kT{i}", [128, 64], BF16)) for i in range(2)]
                t_qeB = [RE(nc.sbuf_tensor(f"a_qeB{i}", [128, 64], BF16)) for i in range(8)]
                t_kme = [RE(nc.sbuf_tensor(f"a_kme{i}", [128, 64], BF16)) for i in range(2)]
                t_kdT = [RE(nc.sbuf_tensor(f"a_kdT{i}", [128, 64], BF16)) for i in range(2)]
                t_kd = [RE(nc.sbuf_tensor(f"a_kd{i}", [64, 128], BF16)) for i in range(8)]
                t_AT = [RE(nc.sbuf_tensor(f"a_AT{i}", [64, 64], BF16)) for i in range(8)]
                t_dg = [RE(nc.sbuf_tensor(f"a_dg{i}", [128, 128], BF16)) for i in range(8)]
                t_A2 = [RE(nc.sbuf_tensor(f"a_A2{i}", [64, 64], BF16)) for i in range(4)]
                t_qe2 = [RE(nc.sbuf_tensor(f"a_qe2{i}", [128, 64], BF16)) for i in range(4)]
                t_sm = RE(nc.sbuf_tensor("a_sm", [64, 24], F32))
                t_yT2 = RE(nc.sbuf_tensor("a_yT2", [128, NHC * STRIPE], BF16))
                t_rr = t_aoe[0]
                t_rsb = t_aoe[1]
                yT_of = lambda s: (t_yT if s % 2 == 0 else t_yT2)

                P.op("vector", "memset", t_Sb[:], 0.0, inc=sDVE)
                x_free = [None, None]
                vh_free = [None] * 8
                slot_guard = [None] * 8
                wt_busy = {}
                st_q = {"q_act": 0, "k_act": 0, "sup": [0, 0, 0, 0], "sb": 0, "oT_ev": 0,
                        "epi_act": 0, "pf": 0}
                pg_guard = {0: {}, 1: {}}
                prep_out = {}

                def load_x_chunk(gch):
                    slot = gch % 2
                    sl = SL[f"xat{slot}"]
                    if x_free[slot] is not None:
                        P.wait("scalar", x_free[slot][0], x_free[slot][1])
                    P.op("scalar", "dma_start", out=t_xat[slot][:],
                         in_=x1s[gch * 64:(gch + 1) * 64, :], inc=sl, inck=16)
                    return (sl, sl.n)

                vh_tr_of = {}

                def emit_prep_stages(gch):
                    xld = load_x_chunk(gch)
                    p = gch % 2
                    s, ch = divmod(gch, NCH)
                    G = pg_guard[p]
                    st = p * 8
                    P.wait("vector", xld[0], xld[1])
                    P.wait("scalar", xld[0], xld[1])
                    if vh_free[ch] is not None:
                        P.wait("scalar", vh_free[ch][0], vh_free[ch][1])
                    if gch > 0:
                        P.wait("scalar", sPE, vh_tr_of[gch - 1])
                    emit_ln(t_xat[p][:], t_vh[ch][:], t_sq6[0:64, :], t_sm, st, 64)
                    vh_rdy = sACT.n
                    x_free[p] = (sACT, vh_rdy)
                    yield
                    P.wait("tensor", sACT, vh_rdy)
                    yTs = yT_of(s)
                    transpose_128(
                        t_vh[ch],
                        lambda hcc, ch=ch, yTs=yTs: yTs[:, hcc * STRIPE + ch * 64: hcc * STRIPE + ch * 64 + 64],
                        NHC, t_i64b[:], 64)
                    vh_tr = sPE.n
                    vh_tr_of[gch] = vh_tr
                    P.wait("tensor", sDVE, max(st_q["q_act"], st_q["k_act"]))
                    for hc in range(NHC):
                        P.op("tensor", "matmul", pA[0][0:64, 0:128],
                             yTs[:, hc * STRIPE + ch * 64: hc * STRIPE + ch * 64 + 64],
                             t_wqT[:, hc * DK:(hc + 1) * DK],
                             start=(hc == 0), stop=(hc == NHC - 1),
                             inc=(sPE if hc == NHC - 1 else None))
                        P.op("tensor", "matmul", pA[1][0:64, 0:128],
                             yTs[:, hc * STRIPE + ch * 64: hc * STRIPE + ch * 64 + 64],
                             t_wkT[:, hc * DK:(hc + 1) * DK],
                             start=(hc == 0), stop=(hc == NHC - 1),
                             inc=(sPE if hc == NHC - 1 else None))
                    kz_mm = sPE.n
                    qz_mm = kz_mm - 1
                    P.wait("vector", sPE, kz_mm)
                    if "qzf_r" in G:
                        P.wait("vector", sACT, G["qzf_r"])
                    P.op("vector", "tensor_copy", t_nb[p][:], pA[0][0:64, 0:128])
                    P.op("vector", "tensor_copy", t_kzf[p][:], pA[1][0:64, 0:128], inc=sDVE)
                    st_q["q_act"] = sDVE.n
                    st_q["k_act"] = sDVE.n
                    qkz_ev = sDVE.n
                    P.wait("scalar", sDVE, qkz_ev)
                    if "qz_w" in G:
                        P.wait("scalar", G["qz_w"][0], G["qz_w"][1])
                    P.op("scalar", "activation", t_qz[p][:], t_nb[p][:], AF.Exp,
                         accum_out=t_sm[:, 17:18], inc=sACT)
                    G["qzf_r"] = sACT.n
                    P.wait("vector", sACT, sACT.n)
                    P.op("vector", "reciprocal", t_sm[:, 18:19], t_sm[:, 17:18])
                    P.drain("vector")
                    if "qb_r" in G:
                        P.wait("vector", sPE, G["qb_r"])
                    P.op("vector", "tensor_scalar_mul", t_qb[p][:], t_qz[p][:], t_sm[:, 18:19],
                         inc=sDVE)
                    q_rdy = (sDVE, sDVE.n)
                    G["qz_w"] = q_rdy
                    if "e1_w" in G:
                        P.wait("scalar", sDVE, G["e1_w"])
                    P.op("scalar", "activation", t_e1[p][:], t_kzf[p][:], AF.Exp,
                         scale=-1.0, inc=sACT)
                    P.wait("vector", sACT, sACT.n)
                    if "sp_r" in G:
                        P.wait("vector", G["sp_r"][0], G["sp_r"][1])
                    P.drain("vector")
                    P.op("vector", "tensor_scalar_add", t_kf[p][:], t_e1[p][:], 1.0)
                    P.drain("vector")
                    P.op("vector", "reciprocal", t_kf[p][:], t_kf[p][:], inc=sDVE)
                    G["e1_w"] = sDVE.n
                    k_rdy = sDVE.n
                    P.wait("gpsimd", sDVE, k_rdy)
                    if "qb_r" in G:
                        P.wait("gpsimd", sPE, G["qb_r"])
                    P.drain("gpsimd")
                    P.op("gpsimd", "tensor_copy", t_kb[p][:], t_kf[p][:], inc=sPL)
                    kb_rdy = (sPL, sPL.n)
                    P.wait("scalar", sDVE, k_rdy)
                    P.op("scalar", "activation", t_e1[p][:], t_kf[p][:], AF.Exp, scale=-1.0)
                    P.drain("scalar")
                    P.op("scalar", "activation", t_e1[p][:], t_e1[p][:], AF.Ln,
                         bias=t_cm[0:64, 1:2], inc=sACT)
                    P.wait("gpsimd", sACT, sACT.n)
                    if "sp_mm" in G:
                        P.wait("gpsimd", sPE, G["sp_mm"])
                    P.drain("gpsimd")
                    P.op("gpsimd", "tensor_add", t_sp[p][:], t_kf[p][:], t_e1[p][:], inc=sPL)
                    sp_rdy = (sPL, sPL.n)
                    G["sp_r"] = sp_rdy
                    yield
                    P.wait("tensor", sp_rdy[0], sp_rdy[1])
                    P.wait("tensor", sDVE, st_q["pf"])
                    P.op("tensor", "matmul", pF[:, 128:192], t_sp[p][:], t_triu[:],
                         start=True, stop=True, inc=sPE)
                    G["sp_mm"] = sPE.n
                    P.wait("vector", sPE, sPE.n)
                    if "ed_r" in G:
                        P.wait("vector", sACT, G["ed_r"])
                    P.op("vector", "tensor_copy", t_nbT[p][:], pF[:, 128:192], inc=sDVE)
                    nbT_rdy = sDVE.n
                    st_q["pf"] = nbT_rdy
                    P.wait("scalar", sDVE, nbT_rdy)
                    if "sc_r" in G:
                        P.wait("scalar", G["sc_r"][0], G["sc_r"][1])
                    P.op("scalar", "activation", t_eB[p][:], t_nbT[p][:], AF.Exp, scale=-1.0)
                    P.drain("scalar")
                    P.op("scalar", "activation", t_emB[p][:], t_nbT[p][:], AF.Exp, inc=sACT)
                    exps_rdy = sACT.n
                    P.wait("vector", sDVE, nbT_rdy)
                    P.drain("vector")
                    P.op("vector", "tensor_scalar", t_ed[p][:], t_nbT[p][:], t_nbT[p][:, 63:64],
                         None, ALU.subtract, inc=sDVE)
                    P.wait("scalar", sDVE, sDVE.n)
                    if slot_guard[ch] is not None:
                        P.wait("scalar", sDVE, slot_guard[ch]["stt"])
                    P.drain("scalar")
                    P.op("scalar", "activation", t_ed[p][:], t_ed[p][:], AF.Exp)
                    P.drain("scalar")
                    P.op("scalar", "activation", t_eld[ch][:], t_nbT[p][:, 63:64], AF.Exp,
                         scale=-1.0, inc=sACT)
                    ed_rdy = sACT.n
                    G["ed_r"] = ed_rdy
                    P.wait("vector", sACT, ed_rdy)
                    if slot_guard[ch] is not None:
                        P.wait("vector", sPE, slot_guard[ch]["mm"])
                    P.drain("vector")
                    P.op("vector", "tensor_scalar_mul", t_dg[ch][:], t_i128b[:],
                         t_eld[ch][:, 0:1], inc=sDVE)
                    dg_rdy = sDVE.n
                    P.wait("tensor", q_rdy[0], q_rdy[1])
                    P.wait("tensor", kb_rdy[0], kb_rdy[1])
                    P.wait("tensor", sDVE, ptfree[0])
                    P.op("tensor", "transpose", pT[:, 0:64], t_qb[p][:], t_i64b[:])
                    P.op("tensor", "transpose", pT[:, 64:128], t_kb[p][:], t_i64b[:], inc=sPE)
                    G["qb_r"] = sPE.n
                    P.wait("vector", sPE, sPE.n)
                    if "qT_r" in G:
                        P.wait("vector", G["qT_r"][0], G["qT_r"][1])
                    P.op("vector", "tensor_copy", t_qT[p][:], pT[:, 0:64])
                    P.op("vector", "tensor_copy", t_kT[p][:], pT[:, 64:128], inc=sDVE)
                    qkT_rdy = sDVE.n
                    ptfree[0] = qkT_rdy
                    P.wait("tensor", sDVE, qkT_rdy)
                    P.wait("gpsimd", sDVE, qkT_rdy)
                    P.wait("gpsimd", sACT, exps_rdy)
                    if slot_guard[ch] is not None:
                        P.wait("gpsimd", sPE, slot_guard[ch]["mm"])
                    P.drain("gpsimd")
                    P.op("gpsimd", "tensor_mul", t_qeB[ch][:], t_qT[p][:], t_eB[p][:])
                    P.drain("gpsimd")
                    P.op("gpsimd", "tensor_mul", t_kme[p][:], t_kT[p][:], t_emB[p][:], inc=sPL)
                    P.wait("gpsimd", sACT, ed_rdy)
                    P.wait("gpsimd", sPE, st_q.get("a2_mm", 0))
                    P.drain("gpsimd")
                    P.op("gpsimd", "tensor_mul", t_kdT[p][:], t_kT[p][:], t_ed[p][:], inc=sPL)
                    scaled_rdy = (sPL, sPL.n)
                    G["sc_r"] = scaled_rdy
                    G["qT_r"] = scaled_rdy
                    yield
                    P.wait("tensor", scaled_rdy[0], scaled_rdy[1])
                    P.wait("tensor", sDVE, st_q["pf"])
                    P.op("tensor", "matmul", pF[0:64, 192:256], t_kme[p][:], t_qeB[ch][:],
                         start=True, stop=True, inc=sPE)
                    at_mm = sPE.n
                    P.op("tensor", "transpose", pT[0:64, 0:128], t_kdT[p][:], t_i128b[:], inc=sPE)
                    kd_mm = sPE.n
                    P.wait("vector", sPE, at_mm)
                    if slot_guard[ch] is not None:
                        P.wait("vector", sPE, slot_guard[ch]["mm"])
                    P.op("vector", "tensor_mul", t_AT[ch][:], pF[0:64, 192:256], t_triu[:],
                         inc=sDVE)
                    at_rdy = sDVE.n
                    st_q["pf"] = at_rdy
                    P.wait("vector", sPE, kd_mm)
                    if slot_guard[ch] is not None:
                        P.wait("vector", sPE, slot_guard[ch]["mm"])
                    P.op("vector", "tensor_copy", t_kd[ch][:], pT[0:64, 0:128], inc=sDVE)
                    kd_rdy = sDVE.n
                    ptfree[0] = kd_rdy
                    a2_rdy = 0
                    if ch % 2 == 1:
                        p2 = (ch - 1) % 2
                        P.wait("tensor", sDVE, st_q["pf"])
                        P.op("tensor", "matmul", pF[0:64, 256:320], t_kdT[p2][:], t_qeB[ch][:],
                             start=True, stop=True, inc=sPE)
                        a2_mm = sPE.n
                        st_q["a2_mm"] = a2_mm
                        P.wait("vector", sPE, a2_mm)
                        if slot_guard[ch] is not None:
                            P.wait("vector", sPE, slot_guard[ch]["mm"])
                        P.op("vector", "tensor_copy", t_A2[ch // 2][:], pF[0:64, 256:320], inc=sDVE)
                        st_q["pf"] = sDVE.n
                        P.drain("vector")
                        P.op("vector", "tensor_scalar_mul", t_qe2[ch // 2][:], t_qeB[ch][:],
                             t_eld[ch - 1][:, 0:1], inc=sDVE)
                        a2_rdy = sDVE.n
                    prep_out[gch] = {"vh_tr": vh_tr, "at_rdy": at_rdy, "kd_rdy": kd_rdy,
                                     "ed_rdy": ed_rdy, "a2_rdy": a2_rdy, "dg_rdy": dg_rdy}

                def emit_body_pair(gch):
                    s, ch0 = divmod(gch, NCH)
                    ch1 = ch0 + 1
                    po0 = prep_out[gch]
                    po1 = prep_out[gch + 1]
                    P.wait("tensor", sDVE, po0["kd_rdy"])
                    P.wait("tensor", sDVE, po0["at_rdy"])
                    P.wait("tensor", sDVE, po1["kd_rdy"])
                    P.wait("tensor", sDVE, po1["at_rdy"])
                    P.wait("tensor", sDVE, po1["a2_rdy"])
                    if st_q["sb"]:
                        P.wait("tensor", sDVE, st_q["sb"])
                    if st_q["oT_ev"]:
                        P.wait("tensor", sDVE, st_q["oT_ev"])
                    for m8 in range(8):
                        for half in range(2):
                            m = m8 + 8 * half
                            P.op("tensor", "matmul", pB[half][:, m8 * 64:m8 * 64 + 64],
                                 t_vh[ch0][:, m * 128:(m + 1) * 128], t_AT[ch0][:],
                                 start=True, stop=False)
                        for half in range(2):
                            m = m8 + 8 * half
                            P.op("tensor", "matmul", pB[half][:, m8 * 64:m8 * 64 + 64],
                                 t_Sb[:, m * 128:(m + 1) * 128], t_qeB[ch0][:],
                                 start=False, stop=True,
                                 inc=(sPE if m8 == 7 else None))
                    oT0 = [sPE.n - 1, sPE.n]
                    for m8 in range(8):
                        for half in range(2):
                            m = m8 + 8 * half
                            P.op("tensor", "matmul", pA[2 + half][:, m8 * 64:m8 * 64 + 64],
                                 t_vh[ch1][:, m * 128:(m + 1) * 128], t_AT[ch1][:],
                                 start=True, stop=False)
                        for half in range(2):
                            m = m8 + 8 * half
                            P.op("tensor", "matmul", pA[2 + half][:, m8 * 64:m8 * 64 + 64],
                                 t_vh[ch0][:, m * 128:(m + 1) * 128], t_A2[ch1 // 2][:],
                                 start=False, stop=False)
                        for half in range(2):
                            m = m8 + 8 * half
                            P.op("tensor", "matmul", pA[2 + half][:, m8 * 64:m8 * 64 + 64],
                                 t_Sb[:, m * 128:(m + 1) * 128], t_qe2[ch1 // 2][:],
                                 start=False, stop=True,
                                 inc=(sPE if m8 == 7 else None))
                    oT1 = [sPE.n - 1, sPE.n]
                    if ch0 == 0 and st_q["epi_act"]:
                        P.wait("vector", sACT, st_q["epi_act"])
                    for half in range(2):
                        P.wait("vector", sPE, oT0[half])
                        P.op("vector", "tensor_copy",
                             t_oT[:].rearrange("p (m t) -> p m t", m=NHC)[:, half * 8:half * 8 + 8,
                                                                          ch0 * 64:(ch0 + 1) * 64],
                             pB[half][:].rearrange("p (m t) -> p m t", m=8), inc=sDVE)
                    ev0 = sDVE.n
                    for half in range(2):
                        P.wait("vector", sPE, oT1[half])
                        P.op("vector", "tensor_copy",
                             t_oT[:].rearrange("p (m t) -> p m t", m=NHC)[:, half * 8:half * 8 + 8,
                                                                          ch1 * 64:(ch1 + 1) * 64],
                             pA[2 + half][:].rearrange("p (m t) -> p m t", m=8), inc=sDVE)
                    ev1 = sDVE.n
                    st_q["oT_ev"] = ev1
                    kv0_banks = [pA[0], pA[1], pF, pB[0]]
                    kv1_banks = [pB[1], pA[2], pA[3], pA[0]]
                    P.wait("tensor", sDVE, st_q["q_act"])
                    P.wait("tensor", sDVE, st_q["pf"])
                    P.wait("tensor", sDVE, ev0)
                    P.wait("tensor", sDVE, po0["dg_rdy"])
                    for n in range(4):
                        P.op("tensor", "matmul", kv0_banks[n][:, 0:512], t_dg[ch0][:],
                             t_Sb[:, n * 512:(n + 1) * 512], start=True, stop=False)
                    for n in range(4):
                        P.op("tensor", "matmul", kv0_banks[n][:, 0:512], t_kd[ch0][:],
                             t_vh[ch0][:, n * 512:(n + 1) * 512], start=False, stop=True,
                             inc=(sPE if n == 3 else None))
                    kv0 = sPE.n
                    P.wait("vector", sPE, kv0)
                    P.drain("vector")
                    s_up0 = []
                    for n in range(4):
                        P.op("vector", "tensor_copy", t_Sb[:, n * 512:(n + 1) * 512],
                             kv0_banks[n][:, 0:512], inc=sDVE)
                        s_up0.append(sDVE.n)
                    P.wait("tensor", sDVE, s_up0[3])
                    P.wait("tensor", sDVE, ev1)
                    P.wait("tensor", sDVE, po1["dg_rdy"])
                    for n in range(4):
                        P.op("tensor", "matmul", kv1_banks[n][:, 0:512], t_dg[ch1][:],
                             t_Sb[:, n * 512:(n + 1) * 512], start=True, stop=False)
                    for n in range(4):
                        P.op("tensor", "matmul", kv1_banks[n][:, 0:512], t_kd[ch1][:],
                             t_vh[ch1][:, n * 512:(n + 1) * 512], start=False, stop=True,
                             inc=(sPE if n == 3 else None))
                    kv1b = sPE.n
                    P.wait("vector", sPE, kv1b)
                    P.drain("vector")
                    s_up1 = []
                    for n in range(4):
                        P.op("vector", "tensor_copy", t_Sb[:, n * 512:(n + 1) * 512],
                             kv1_banks[n][:, 0:512], inc=sDVE)
                        s_up1.append(sDVE.n)
                    st_q["sup"] = [s_up1[3]] * 4
                    st_q["sb"] = s_up1[3]
                    st_q["pf"] = s_up1[3]
                    st_q["q_act"] = s_up1[3]
                    st_q["k_act"] = s_up1[3]
                    vh_free[ch0] = (sDVE, s_up1[3])
                    vh_free[ch1] = (sDVE, s_up1[3])
                    slot_guard[ch0] = {"mm": kv1b, "stt": s_up1[3]}
                    slot_guard[ch1] = {"mm": kv1b, "stt": s_up1[3]}

                epi = {}

                def emit_rms(e):
                    sq_prev = 0
                    P.wait("vector", sDVE, st_q["oT_ev"])
                    P.wait("gpsimd", sDVE, st_q["oT_ev"])
                    P.wait("tensor", sDVE, sDVE.n)
                    mm_done = {}
                    for mc in range(NHC):
                        eng, sem = (("vector", sDVE) if mc % 2 == 0 else ("gpsimd", sPL))
                        if mc >= 2:
                            P.wait(eng, sPE, mm_done[mc - 2])
                        P.drain(eng)
                        P.op(eng, "tensor_mul", t_sgt[mc % 2][:],
                             t_oT[:, mc * STRIPE:(mc + 1) * STRIPE],
                             t_oT[:, mc * STRIPE:(mc + 1) * STRIPE], inc=sem)
                        P.wait("tensor", sem, sem.n)
                        if mc == 0:
                            P.wait("tensor", sDVE, st_q["sup"][3])
                        P.op("tensor", "matmul", pA[2][0:1, 0:512],
                             t_ones[:, 0:1], t_sgt[mc % 2][:],
                             start=(mc == 0), stop=(mc == NHC - 1), inc=sPE)
                        mm_done[mc] = sPE.n
                    rms_mm = sPE.n
                    P.wait("vector", sPE, rms_mm)
                    P.wait("vector", SL["ao0"], SL["ao0"].n)
                    P.op("vector", "tensor_scalar_mul", t_rr[0:1, 0:512], pA[2][0:1, 0:512], 1.0 / D,
                         inc=sDVE)
                    rr_ev = sDVE.n
                    P.wait("scalar", sDVE, rr_ev)
                    P.op("scalar", "activation", t_rr[0:1, 0:512], t_rr[0:1, 0:512], AF.Ln,
                         bias=t_cm[0:1, 0:1])
                    P.drain("scalar")
                    P.op("scalar", "activation", t_rr[0:1, 0:512], t_rr[0:1, 0:512], AF.Exp,
                         scale=-0.5, inc=sACT)
                    P.wait("vector", sACT, sACT.n)
                    P.drain("vector")
                    P.op("vector", "tensor_copy", t_sgt[0][0:1, 0:512], t_rr[0:1, 0:512], inc=sDVE)
                    P.wait("tensor", sDVE, sDVE.n)
                    P.op("tensor", "matmul", pA[2][:, 0:512], t_ones[0:1, :], t_sgt[0][0:1, 0:512],
                         start=True, stop=True, inc=sPE)
                    P.wait("vector", sPE, sPE.n)
                    P.wait("vector", SL["ao1"], SL["ao1"].n)
                    P.op("vector", "tensor_copy", t_rsb[:], pA[2][:, 0:512], inc=sDVE)
                    epi["rsb"] = sDVE.n
                    epi["go_busy"] = {}
                    epi["z_done"] = {}

                def issue_go_pair(mcp):
                    half = mcp % 2
                    for wname, wsrc in (("g", wogT), ("u", wigT)):
                        for wslot in (0, 1):
                            mc = 2 * mcp + wslot
                            key = (wname, wslot, half)
                            if key in wt_busy:
                                P.wait("sync", wt_busy[key][0], wt_busy[key][1])
                            buf = (t_wg if wname == "g" else t_wu)[wslot]
                            sl = SL[("awg" if wname == "g" else "awu") + str(wslot + 2 * half)]
                            P.op("sync", "dma_start",
                                 out=buf[:, half * 2048:half * 2048 + 2048],
                                 in_=wsrc[:, mc * 2048:(mc + 1) * 2048],
                                 inc=sl, inck=16)

                def emit_goT_pair(e, mcp):
                    go_busy = epi["go_busy"]
                    z_done = epi["z_done"]
                    half = mcp % 2
                    co = half * 2048
                    mcs = (2 * mcp, 2 * mcp + 1)
                    if mcp == 0:
                        issue_go_pair(0)
                    if mcp + 1 < NHC // 2:
                        issue_go_pair(mcp + 1)
                    for wslot in (0, 1):
                        for nmw in ("awg", "awu"):
                            sl = SL[nmw + str(wslot + 2 * half)]
                            P.wait("tensor", sl, sl.n)
                    gbank = (pA[2], pA[3]) if mcp % 2 == 0 else (pB[0], pB[1])
                    if mcp == 0:
                        P.wait("tensor", sDVE, epi["rsb"])
                        P.wait("tensor", sDVE, st_q["sup"][3])
                    elif mcp == 1:
                        P.wait("tensor", sDVE, st_q["sup"][3])
                    else:
                        P.wait("tensor", sACT, go_busy[2 * mcp - 4])
                        P.wait("tensor", sACT, go_busy[2 * mcp - 3])
                    for hc in range(NHC):
                        for wslot in (0, 1):
                            P.op("tensor", "matmul", gbank[wslot][:, 0:512],
                                 t_wg[wslot][:, co + hc * 128: co + (hc + 1) * 128],
                                 t_oT[:, hc * STRIPE:(hc + 1) * STRIPE],
                                 start=(hc == 0), stop=False)
                    for hc in range(NHC):
                        for wslot in (0, 1):
                            P.op("tensor", "matmul", gbank[wslot][:, 0:512],
                                 t_wu[wslot][:, co + hc * 128: co + (hc + 1) * 128],
                                 yT_of(e)[:, hc * STRIPE:(hc + 1) * STRIPE],
                                 start=False, stop=(hc == NHC - 1),
                                 inc=(sPE if hc == NHC - 1 else None))
                    go_mm = sPE.n
                    for wname in ("g", "u"):
                        for wslot in (0, 1):
                            wt_busy[(wname, wslot, half)] = (sPE, go_mm)
                    for wslot in (0, 1):
                        mc = mcs[wslot]
                        P.wait("scalar", sPE, go_mm)
                        if mc >= 2:
                            P.wait("scalar", sDVE, z_done[mc - 2])
                        P.op("scalar", "activation", t_sgt[wslot][:], gbank[wslot][:, 0:512],
                             AF.Silu, inc=sACT)
                        go_busy[mc] = sACT.n
                        P.wait("vector", sACT, go_busy[mc])
                        P.wait("vector", sDVE, epi["rsb"])
                        P.wait("vector", SL["ao0"], SL["ao0"].n)
                        P.drain("vector")
                        P.op("vector", "scalar_tensor_tensor", t_zt[:],
                             t_oT[:, mc * STRIPE:(mc + 1) * STRIPE], t_gnw[:, mc:mc + 1], t_rsb[:],
                             ALU.mult, ALU.mult)
                        P.drain("vector")
                        P.op("vector", "tensor_mul", t_z[:, mc * STRIPE:(mc + 1) * STRIPE],
                             t_zt[:], t_sgt[wslot][:], inc=sDVE)
                        z_done[mc] = sDVE.n
                    if mcp == NHC // 2 - 1:
                        epi["zT_all"] = sDVE.n
                        st_q["epi_act"] = sACT.n

                def emit_ao(e, prep_cb=None):
                    go_busy = epi["go_busy"]
                    P.wait("tensor", sDVE, epi["zT_all"])
                    ao_psums = [pB[0], pB[1], pA[2], pA[3]]
                    ao_ev_done = 0
                    for mn in range(4):
                        if mn == 0:
                            P.wait("tensor", sDVE, max(st_q["oT_ev"], st_q["sup"][3]))
                            P.wait("tensor", sACT, go_busy[NHC - 1])
                        else:
                            P.wait("tensor", sDVE, ao_ev_done)
                        for dvc in range(NHC):
                            wslot2 = dvc % 2
                            key = ("wo", wslot2)
                            if key in wt_busy:
                                P.wait("sync", wt_busy[key][0], wt_busy[key][1])
                            sl = SL[f"wo{wslot2}"]
                            P.op("sync", "dma_start", out=t_wot[wslot2][:, 0:512],
                                 in_=woT[dvc * 128:(dvc + 1) * 128, mn * 512:(mn + 1) * 512],
                                 inc=sl, inck=16)
                            P.wait("tensor", sl, sl.n)
                            for tt in range(NTT):
                                P.op("tensor", "matmul", ao_psums[tt][:, 0:512],
                                     t_z[:, dvc * STRIPE + tt * 128: dvc * STRIPE + tt * 128 + 128],
                                     t_wot[wslot2][:, 0:512],
                                     start=(dvc == 0), stop=(dvc == NHC - 1),
                                     inc=(sPE if tt == NTT - 1 else None))
                            wt_busy[key] = (sPE, sPE.n)
                        ao_mm = sPE.n
                        for tt in range(NTT):
                            aslot = tt % 2
                            sl = SL[f"ao{aslot}"]
                            P.wait("vector", sl, sl.n)
                            P.wait("vector", sPE, ao_mm)
                            P.drain("vector")
                            P.op("vector", "tensor_copy", t_aoe[aslot][:], ao_psums[tt][:, 0:512],
                                 inc=sDVE)
                            P.wait("scalar", sDVE, sDVE.n)
                            P.op("scalar", "dma_start",
                                 out=aos[(e * NTT + tt) * 128:(e * NTT + tt + 1) * 128,
                                         mn * 512:(mn + 1) * 512],
                                 in_=t_aoe[aslot][:], inc=sl, inck=16)
                        ao_ev_done = sDVE.n
                        if prep_cb is not None:
                            prep_cb(mn)
                    st_q["sup"] = [ao_ev_done] * 4
                    st_q["oT_ev"] = ao_ev_done

                def pump(gens):
                    alive = []
                    for g in gens:
                        try:
                            next(g)
                            alive.append(g)
                        except StopIteration:
                            pass
                    gens[:] = alive

                gens = []
                for ch in range(NCH):
                    pump(gens)
                    g = emit_prep_stages(ch)
                    next(g)
                    gens.append(g)
                while gens:
                    pump(gens)
                for ch in range(0, NCH, 2):
                    emit_body_pair(ch)
                for s in range(1, NS):
                    emit_rms(s - 1)
                    gens = []
                    nxt = [0]

                    def step(s=s, gens=gens, nxt=nxt):
                        pump(gens)
                        if nxt[0] < NCH:
                            g = emit_prep_stages(s * NCH + nxt[0])
                            nxt[0] += 1
                            next(g)
                            gens.append(g)
                    step()
                    for k in range(NHC // 2):
                        emit_goT_pair(s - 1, k)
                        step()
                    emit_ao(s - 1, prep_cb=lambda mn, f=step: f())
                    while gens:
                        pump(gens)
                    for ch in range(0, NCH, 2):
                        emit_body_pair(s * NCH + ch)
                emit_rms(NS - 1)
                for k in range(NHC // 2):
                    emit_goT_pair(NS - 1, k)
                emit_ao(NS - 1)

        if 1 in phases:
            ffn_phase(1)
        if 2 in phases:
            barrier()
            attn_phase()
        if 3 in phases:
            barrier()
            ffn_phase(2)
        for k in ("xa0", "xa1", "xb0", "xb1"):
            P.wait("scalar", SL[k], SL[k].n)

        block = E(nc.Block())

        @block.sync
        def _(e):
            P.replay(e, "sync")

        @block.scalar
        def _(e):
            P.replay(e, "scalar")

        @block.vector
        def _(e):
            P.replay(e, "vector")

        @block.tensor
        def _(e):
            P.replay(e, "tensor")

        @block.gpsimd
        def _(e):
            P.replay(e, "gpsimd")

    return nc


def _block_gu(w):
    return np.ascontiguousarray(
        w.reshape(NHC, 128, INTER // 256, 256).transpose(1, 2, 0, 3).reshape(128, -1))


def _block_wd(w):
    return np.ascontiguousarray(
        w.reshape(INTER // 512, 4, 128, 4, 512).transpose(2, 3, 0, 1, 4).reshape(128, -1))


def make_consts():
    return dict(
        ci64b=np.eye(64, dtype=np.float32).astype(bf16),
        ci128b=np.eye(128, dtype=np.float32).astype(bf16),
        ci64f=np.eye(64, dtype=np.float32),
        ctriu=np.triu(np.ones((64, 64), np.float32)),
        cones=np.ones((128, 128), np.float32).astype(bf16),
        cmisc=np.tile(np.array([[EPS, 1.0]], np.float32), (128, 1)),
    )


def kernel(L0_attn_hidden_states, L0_attn_Wq, L0_attn_Wk, L0_attn_Wog, L0_attn_Wig,
           L0_attn_Wo, L0_attn_g_norm_w, L0_ffn_w_gate, L0_ffn_w_up, L0_ffn_w_down,
           L0_ffn_s_gate, L0_ffn_s_up, L0_ffn_s_down):
    nc = build_program()
    shared = dict(
        wq=np.ascontiguousarray(L0_attn_Wq), wk=np.ascontiguousarray(L0_attn_Wk),
        wog=np.ascontiguousarray(L0_attn_Wog), wig=np.ascontiguousarray(L0_attn_Wig),
        wo=np.ascontiguousarray(L0_attn_Wo), gnw=np.ascontiguousarray(L0_attn_g_norm_w),
        wg8b=_block_gu(L0_ffn_w_gate), wu8b=_block_gu(L0_ffn_w_up),
        wd8b=_block_wd(L0_ffn_w_down),
        sg=np.ascontiguousarray(L0_ffn_s_gate), su=np.ascontiguousarray(L0_ffn_s_up),
        sd=np.ascontiguousarray(L0_ffn_s_down), **make_consts(),
    )
    in_maps = [dict(x0=np.ascontiguousarray(L0_attn_hidden_states[c]), **shared)
               for c in range(8)]
    res = run_bass_kernel_spmd(nc, in_maps, core_ids=list(range(8)))
    return np.stack([res.results[c]["out"] for c in range(8)], axis=0)


# version 26
# speedup vs baseline: 1.0154x; 1.0154x over previous
import numpy as np
import ml_dtypes
from contextlib import ExitStack
import concourse.bass as bass
import concourse.mybir as mybir
from concourse.bass_utils import run_bass_kernel_spmd

F32 = mybir.dt.float32
BF16 = mybir.dt.bfloat16
I8 = mybir.dt.int8
AF = mybir.ActivationFunctionType
ALU = mybir.AluOpType
AX = mybir.AxisListType
bf16 = ml_dtypes.bfloat16

T = 2048
D = 2048
INTER = 8192
STRIPE = 512
NS = T // STRIPE
NTT = STRIPE // 128
C = 64
NCH = STRIPE // C
DK = 128
EPS = 1e-5
NHC = D // 128


class Sem:
    def __init__(self, nc, ctx, name):
        self.h = ctx.enter_context(nc.semaphore(name))
        self.n = 0


class Prog:
    ENGINES = ("sync", "scalar", "vector", "tensor", "gpsimd")

    def __init__(self):
        self.q = {e: [] for e in self.ENGINES}

    def op(self, eng, name, *args, inc=None, inck=1, **kw):
        if inc is not None:
            inc.n += inck
        self.q[eng].append(("op", name, args, kw, (inc.h if inc else None), inck))
        return inc.n if inc else None

    def wait(self, eng, sem, val):
        self.q[eng].append(("wait", sem.h, val))

    def drain(self, eng):
        self.q[eng].append(("drain",))

    def replay(self, eng_obj, name):
        for item in self.q[name]:
            if item[0] == "wait":
                eng_obj.wait_ge(item[1], item[2])
            elif item[0] == "drain":
                eng_obj.drain()
            else:
                _, opname, args, kw, semh, inck = item
                ins = getattr(eng_obj, opname)(*args, **kw)
                if semh is not None:
                    ins.then_inc(semh, inck)


def build_program(phases=(1, 2, 3)):
    nc = bass.Bass("TRN2", target_bir_lowering=False)

    x0 = nc.dram_tensor("x0", [T, D], F32, kind="ExternalInput").ap()
    wq = nc.dram_tensor("wq", [DK, D], F32, kind="ExternalInput").ap()
    wk = nc.dram_tensor("wk", [DK, D], F32, kind="ExternalInput").ap()
    wog = nc.dram_tensor("wog", [D, D], F32, kind="ExternalInput").ap()
    wig = nc.dram_tensor("wig", [D, D], F32, kind="ExternalInput").ap()
    wo = nc.dram_tensor("wo", [D, D], F32, kind="ExternalInput").ap()
    gnw = nc.dram_tensor("gnw", [D], F32, kind="ExternalInput").ap()
    wg8 = nc.dram_tensor("wg8b", [128, (INTER // 256) * NHC * 256], I8, kind="ExternalInput").ap()
    wu8 = nc.dram_tensor("wu8b", [128, (INTER // 256) * NHC * 256], I8, kind="ExternalInput").ap()
    wd8 = nc.dram_tensor("wd8b", [128, (INTER // 256) * 2 * D], I8, kind="ExternalInput").ap()
    sg = nc.dram_tensor("sg", [INTER], F32, kind="ExternalInput").ap()
    su = nc.dram_tensor("su", [INTER], F32, kind="ExternalInput").ap()
    sd = nc.dram_tensor("sd", [D], F32, kind="ExternalInput").ap()
    ci64b = nc.dram_tensor("ci64b", [64, 64], BF16, kind="ExternalInput").ap()
    ci128b = nc.dram_tensor("ci128b", [128, 128], BF16, kind="ExternalInput").ap()
    ci64f = nc.dram_tensor("ci64f", [64, 64], F32, kind="ExternalInput").ap()
    ctriu = nc.dram_tensor("ctriu", [64, 64], F32, kind="ExternalInput").ap()
    cones = nc.dram_tensor("cones", [128, 128], BF16, kind="ExternalInput").ap()
    cmisc = nc.dram_tensor("cmisc", [128, 2], F32, kind="ExternalInput").ap()
    out = nc.dram_tensor("out", [T, D], F32, kind="ExternalOutput").ap()
    x1s = nc.dram_tensor("x1s", [T, D], F32, kind="Internal").ap()
    aos = nc.dram_tensor("aos", [T, D], F32, kind="Internal").ap()
    wogb = nc.dram_tensor("wogb", [D, D], BF16, kind="Internal").ap()
    wigb = nc.dram_tensor("wigb", [D, D], BF16, kind="Internal").ap()
    wob = nc.dram_tensor("wob", [D, D], BF16, kind="Internal").ap()
    wogT = nc.dram_tensor("wogT", [128, NHC * D], BF16, kind="Internal").ap()
    wigT = nc.dram_tensor("wigT", [128, NHC * D], BF16, kind="Internal").ap()
    woT = nc.dram_tensor("woT", [D, D], BF16, kind="Internal").ap()

    P = Prog()
    ctx = ExitStack()
    with ctx:
        E = ctx.enter_context
        sPE = Sem(nc, ctx, "sPE")
        sACT = Sem(nc, ctx, "sACT")
        sDVE = Sem(nc, ctx, "sDVE")
        sBAR = Sem(nc, ctx, "sBAR")
        sPL = Sem(nc, ctx, "sPL")
        dC = Sem(nc, ctx, "dC")
        dP = Sem(nc, ctx, "dP")
        dQ = Sem(nc, ctx, "dQ")
        dT = Sem(nc, ctx, "dT")
        SL = {k: Sem(nc, ctx, "d_" + k) for k in
              ("wg0", "wg1", "wu0", "wu1", "wd0", "wd1",
               "xa0", "xa1", "xb0", "xb1", "awg0", "awg1", "awu0", "awu1",
               "awg2", "awg3", "awu2", "awu3",
               "xat0", "xat1", "wo0", "wo1", "ao0", "ao1")}

        t_i64b = E(nc.sbuf_tensor("t_i64b", [64, 64], BF16))
        t_i128b = E(nc.sbuf_tensor("t_i128b", [128, 128], BF16))
        t_i64f = E(nc.sbuf_tensor("t_i64f", [64, 64], F32))
        t_triu = E(nc.sbuf_tensor("t_triu", [64, 64], F32))
        t_ones = E(nc.sbuf_tensor("t_ones", [128, 128], BF16))
        t_sgp = E(nc.sbuf_tensor("t_sgp", [128, 64], F32))
        t_sup = E(nc.sbuf_tensor("t_sup", [128, 64], F32))
        t_gnw = E(nc.sbuf_tensor("t_gnw", [128, 16], F32))
        t_sdb = E(nc.sbuf_tensor("t_sdb", [128, D], F32))
        t_wqT = E(nc.sbuf_tensor("t_wqT", [128, NHC * DK], BF16))
        t_wkT = E(nc.sbuf_tensor("t_wkT", [128, NHC * DK], BF16))
        t_S = E(nc.sbuf_tensor("t_S", [128, D], F32))
        t_Sb = E(nc.sbuf_tensor("t_Sb", [128, D], BF16))
        t_yT = E(nc.sbuf_tensor("t_yT", [128, NHC * STRIPE], BF16))
        t_wg = [E(nc.sbuf_tensor(f"t_wg{i}", [128, NHC * 256], BF16)) for i in range(2)]
        t_wu = [E(nc.sbuf_tensor(f"t_wu{i}", [128, NHC * 256], BF16)) for i in range(2)]
        t_ptr = E(nc.sbuf_tensor("t_ptr", [128, 512], BF16))
        t_stat = E(nc.sbuf_tensor("t_stat", [128, 16], F32))
        t_cm = E(nc.sbuf_tensor("t_cm", [128, 2], F32))

        pA = [E(nc.psum_tensor(f"pA{i}", [128, 512], F32)) for i in range(4)]
        pB = [E(nc.psum_tensor(f"pB{i}", [128, 512], F32)) for i in range(2)]
        pF = E(nc.psum_tensor("pF", [128, 512], F32))
        pT = E(nc.psum_tensor("pT", [128, 1024], BF16))

        pctx = ExitStack()
        t_sdr = pctx.enter_context(nc.sbuf_tensor("t_sdr", [1, D], BF16))
        t_wqk = pctx.enter_context(nc.sbuf_tensor("t_wqk", [128, D], BF16))
        P.op("scalar", "dma_start", out=t_i64b[:], in_=ci64b[:], inc=dC, inck=16)
        P.op("scalar", "dma_start", out=t_i128b[:], in_=ci128b[:], inc=dC, inck=16)
        P.op("scalar", "dma_start", out=t_i64f[:], in_=ci64f[:], inc=dC, inck=16)
        P.op("scalar", "dma_start", out=t_triu[:], in_=ctriu[:], inc=dC, inck=16)
        P.op("scalar", "dma_start", out=t_ones[:], in_=cones[:], inc=dC, inck=16)
        P.op("scalar", "dma_start", out=t_sgp[:], allow_slow_non_contiguous=True,
             in_=sg.rearrange("(c p) -> p c", p=128), inc=dC, inck=16)
        P.op("scalar", "dma_start", out=t_sup[:], allow_slow_non_contiguous=True,
             in_=su.rearrange("(c p) -> p c", p=128), inc=dC, inck=16)
        P.op("scalar", "dma_start", out=t_gnw[:], allow_slow_non_contiguous=True,
             in_=gnw.rearrange("(c p) -> p c", p=128), inc=dC, inck=16)
        P.op("scalar", "dma_start", out=t_cm[:], in_=cmisc[:], inc=dC, inck=16)
        CONSTS = dC.n

        P.op("gpsimd", "dma_start", out=t_sdr[:], in_=sd.rearrange("(one d) -> one d", one=1),
             inc=dP, inck=16)
        P.op("gpsimd", "dma_start", out=t_wqk[:], in_=wq[:], inc=dP, inck=16)
        P.wait("tensor", dC, CONSTS)
        P.wait("tensor", dP, 32)

        ptfree = [0]

        def transpose_128(src_tile, dst_colf, n_chunks, ident, rows):
            P.wait("tensor", sDVE, ptfree[0])
            for hc in range(n_chunks):
                slotp = hc % 8
                P.op("tensor", "transpose", pT[:, slotp * 128:slotp * 128 + rows],
                     src_tile[:, hc * 128:(hc + 1) * 128], ident, inc=sPE)
                if slotp == 7 or hc == n_chunks - 1:
                    nsl = slotp + 1
                    P.wait("vector", sPE, sPE.n)
                    for h2 in range(nsl):
                        hcc = hc - nsl + 1 + h2
                        P.op("vector", "tensor_copy", dst_colf(hcc),
                             pT[:, h2 * 128:h2 * 128 + rows], inc=sDVE)
                    ptfree[0] = sDVE.n
                    P.wait("tensor", sDVE, sDVE.n)

        transpose_128(t_wqk, lambda hc: t_wqT[:, hc * DK:(hc + 1) * DK], NHC, t_i128b[:], 128)
        P.wait("gpsimd", sDVE, sDVE.n)
        P.op("gpsimd", "dma_start", out=t_wqk[:], in_=wk[:], inc=dP, inck=16)
        P.wait("tensor", dP, 48)
        transpose_128(t_wqk, lambda hc: t_wkT[:, hc * DK:(hc + 1) * DK], NHC, t_i128b[:], 128)

        for dc2 in range(4):
            P.op("tensor", "matmul", pF[:, 0:512], t_ones[0:1, :], t_sdr[:, dc2 * 512:(dc2 + 1) * 512],
                 start=True, stop=True, inc=sPE)
            P.wait("vector", sPE, sPE.n)
            P.op("vector", "tensor_scalar_mul", t_sdb[:, dc2 * 512:(dc2 + 1) * 512], pF[:, 0:512], 0.5,
                 inc=sDVE)
            P.wait("tensor", sDVE, sDVE.n)

        PROLOGUE_DVE = sDVE.n
        pctx.close()
        P.op("gpsimd", "dma_start", out=wogb[:], in_=wog[:], inc=dQ, inck=16)
        P.op("gpsimd", "dma_start", out=wigb[:], in_=wig[:], inc=dQ, inck=16)
        P.op("gpsimd", "dma_start", out=wob[:], in_=wo[:], inc=dQ, inck=16)
        CASTS = dQ.n
        P.wait("sync", dQ, CASTS)
        for srcw, dst, blocked in ((wogb, wogT, True), (wigb, wigT, True), (wob, woT, False)):
            for hc in range(NHC):
                for m4 in range(4):
                    P.op("sync", "dma_start_transpose", t_ptr[:],
                         srcw[m4 * 512:(m4 + 1) * 512, hc * 128:(hc + 1) * 128], inc=dT, inck=16)
                    P.wait("sync", dT, dT.n)
                    if blocked:
                        P.op("sync", "dma_start",
                             out=dst[:].rearrange("p (mc hcb m) -> p mc hcb m", mc=NHC, hcb=NHC)
                                 [:, m4 * 4:(m4 + 1) * 4, hc, :],
                             in_=t_ptr[:].rearrange("p (mcl m) -> p mcl m", mcl=4),
                             inc=dT, inck=16)
                    else:
                        P.op("sync", "dma_start",
                             out=dst[hc * 128:(hc + 1) * 128, m4 * 512:(m4 + 1) * 512],
                             in_=t_ptr[:], inc=dT, inck=16)
                    P.wait("sync", dT, dT.n)
        PREP = dT.n

        def emit_ln(xt, yt, sqt, statt, st, rows, ve="vector", sVE=None):
            if sVE is None:
                sVE = sDVE
            P.op("vector", "reduce_sum", statt[:, st:st + 1], xt, AX.X, inc=sDVE)
            P.op("scalar", "activation", sqt, xt, AF.Square,
                 accum_out=statt[:, st + 1:st + 2], inc=sACT)
            P.wait(ve, sACT, sACT.n)
            if ve != "vector":
                P.wait(ve, sDVE, sDVE.n)
            P.drain(ve)
            P.op(ve, "tensor_scalar_mul", statt[:, st + 2:st + 3], statt[:, st:st + 1], 1.0 / D)
            P.drain(ve)
            P.op(ve, "tensor_scalar", statt[:, st + 3:st + 4], statt[:, st + 2:st + 3],
                 statt[:, st + 2:st + 3], -1.0, ALU.mult, ALU.mult)
            P.drain(ve)
            P.op(ve, "scalar_tensor_tensor", statt[:, st + 4:st + 5], statt[:, st + 1:st + 2],
                 1.0 / D, statt[:, st + 3:st + 4], ALU.mult, ALU.add, inc=sVE)
            P.wait("scalar", sVE, sVE.n)
            P.op("scalar", "activation", statt[:, st + 5:st + 6], statt[:, st + 4:st + 5], AF.Ln,
                 bias=t_cm[0:rows, 0:1])
            P.drain("scalar")
            P.op("scalar", "activation", statt[:, st + 6:st + 7], statt[:, st + 5:st + 6], AF.Exp,
                 scale=-0.5, inc=sACT)
            P.wait(ve, sACT, sACT.n)
            P.op(ve, "tensor_scalar", statt[:, st + 7:st + 8], statt[:, st + 2:st + 3],
                 statt[:, st + 6:st + 7], -1.0, ALU.mult, ALU.mult, inc=sVE)
            P.wait("scalar", sVE, sVE.n)
            P.drain("scalar")
            P.op("scalar", "activation", yt, xt, AF.Identity,
                 bias=statt[:, st + 7:st + 8], scale=statt[:, st + 6:st + 7], inc=sACT)
            return sACT.n

        def ffn_phase(phase):
            xin = x0 if phase == 1 else x1s
            xadd = None if phase == 1 else aos
            xout = x1s if phase == 1 else out
            with ExitStack() as rctx:
                RE = rctx.enter_context
                t_xa = [RE(nc.sbuf_tensor(f"f{phase}_xa{i}", [128, D], F32)) for i in range(4)]
                t_ab = [RE(nc.sbuf_tensor(f"f{phase}_ab0", [128, D], F32))]
                t_fuse = RE(nc.sbuf_tensor(f"f{phase}_fuse", [128, 64 * STRIPE], BF16))
                t_g = [RE(nc.sbuf_tensor(f"f{phase}_g{i}", [128, 512], BF16)) for i in range(2)]

                t_sq = RE(nc.sbuf_tensor(f"f{phase}_sq", [128, D], BF16))
                t_wd = [RE(nc.sbuf_tensor(f"f{phase}_wd{i}", [128, 4 * 512], BF16)) for i in range(2)]

                xa_free = [(sDVE, PROLOGUE_DVE)] * 4 if phase == 1 else [None] * 4
                ab_free = [None]
                g_free = [None, None]
                wbusy = {}
                fuse_ready = {}
                lnyT_ready = {}
                y_sq_free = [0]
                XS = ["xa0", "xa1", "xb0", "xb1"]
                ABS = ["xat0", "xat1"]

                def issue_gu_block(ib):
                    wslot = ib % 2
                    for nm, w8t in (("g", wg8), ("u", wu8)):
                        buf = (t_wg if nm == "g" else t_wu)[wslot]
                        key = (nm, wslot)
                        if key in wbusy:
                            P.wait("gpsimd", wbusy[key][0], wbusy[key][1])
                        sl = SL[("wg" if nm == "g" else "wu") + str(wslot)]
                        P.op("gpsimd", "dma_start",
                             out=buf[:],
                             in_=w8t[:, ib * 4096:(ib + 1) * 4096],
                             inc=sl, inck=16)

                def issue_wd_tile(dc, icg):
                    wdslot = icg % 2
                    key = ("d", wdslot)
                    if key in wbusy:
                        P.wait("gpsimd", wbusy[key][0], wbusy[key][1])
                    P.op("gpsimd", "dma_start",
                         out=t_wd[wdslot][:],
                         in_=wd8[:, (dc * 16 + icg) * 2048:(dc * 16 + icg) * 2048 + 2048],
                         inc=SL[f"wd{wdslot}"], inck=16)

                for s in range(NS):
                    for tt in range(NTT):
                        gt = s * NTT + tt
                        sx = SL[XS[tt]]
                        if xa_free[tt] is not None:
                            P.wait("scalar", xa_free[tt][0], xa_free[tt][1])
                        P.op("scalar", "dma_start", out=t_xa[tt][:],
                             in_=xin[gt * 128:(gt + 1) * 128, :], inc=sx, inck=16)
                        if xadd is not None:
                            sb = SL[ABS[0]]
                            if ab_free[0] is not None:
                                P.wait("scalar", ab_free[0][0], ab_free[0][1])
                            P.op("scalar", "dma_start", out=t_ab[0][:],
                                 in_=xadd[gt * 128:(gt + 1) * 128, :], inc=sb, inck=16)
                            P.wait("vector", sx, sx.n)
                            P.wait("vector", sb, sb.n)
                            P.drain("vector")
                            P.op("vector", "tensor_add", t_xa[tt][:], t_xa[tt][:], t_ab[0][:],
                                 inc=sDVE)
                            ab_free[0] = (sDVE, sDVE.n)
                            rdy = (sDVE, sDVE.n)
                        else:
                            rdy = (sx, sx.n)
                        st = (tt % 2) * 8
                        P.wait("vector", rdy[0], rdy[1])
                        P.wait("scalar", rdy[0], rdy[1])
                        if y_sq_free[0]:
                            P.wait("scalar", sPE, y_sq_free[0])
                        emit_ln(t_xa[tt][:], t_sq[:], t_sq[:], t_stat, st, 128)
                        y_done = sACT.n
                        P.wait("tensor", sACT, y_done)
                        transpose_128(
                            t_sq,
                            lambda hcc, tt=tt: t_yT[:, hcc * STRIPE + tt * 128: hcc * STRIPE + tt * 128 + 128],
                            NHC, t_i128b[:], 128)
                        y_sq_free[0] = sPE.n
                    lnyT_ready[s] = sDVE.n

                    if s == 0:
                        issue_gu_block(0)
                    for ib in range(INTER // 256):
                        wslot = ib % 2
                        if ib + 1 < INTER // 256:
                            issue_gu_block(ib + 1)
                        for icc in range(2):
                            ic = ib * 2 + icc
                            pg = pA[0 + 2 * (ic % 2)]
                            pu = pA[1 + 2 * (ic % 2)]
                            P.wait("tensor", SL[f"wg{wslot}"], SL[f"wg{wslot}"].n)
                            P.wait("tensor", SL[f"wu{wslot}"], SL[f"wu{wslot}"].n)
                            if ic < 2:
                                P.wait("tensor", sDVE, lnyT_ready[s])
                            else:
                                if ic < 4:
                                    P.wait("tensor", sDVE, lnyT_ready[s])
                                P.wait("tensor", sDVE, fuse_ready[(s, ic - 2)])
                            for hc in range(NHC):
                                P.op("tensor", "matmul", pg[:, 0:512],
                                     t_wg[wslot][:, hc * 256 + icc * 128: hc * 256 + icc * 128 + 128],
                                     t_yT[:, hc * STRIPE:(hc + 1) * STRIPE],
                                     start=(hc == 0), stop=(hc == NHC - 1),
                                     inc=(sPE if hc == NHC - 1 else None))
                                P.op("tensor", "matmul", pu[:, 0:512],
                                     t_wu[wslot][:, hc * 256 + icc * 128: hc * 256 + icc * 128 + 128],
                                     t_yT[:, hc * STRIPE:(hc + 1) * STRIPE],
                                     start=(hc == 0), stop=(hc == NHC - 1),
                                     inc=(sPE if hc == NHC - 1 else None))
                            u_mm = sPE.n
                            g_mm = u_mm - 1
                            if icc == 1:
                                wbusy[("g", wslot)] = (sPE, u_mm)
                                wbusy[("u", wslot)] = (sPE, u_mm)
                            eslot = ic % 2
                            P.wait("scalar", sPE, g_mm)
                            if g_free[eslot] is not None:
                                P.wait("scalar", sDVE, g_free[eslot])
                            P.op("scalar", "activation", t_g[eslot][:], pg[:, 0:512], AF.Silu,
                                 scale=t_sgp[:, ic:ic + 1], inc=sACT)
                            P.wait("vector", sACT, sACT.n)
                            P.wait("vector", sPE, u_mm)
                            P.drain("vector")
                            P.op("vector", "scalar_tensor_tensor", t_fuse[:, ic * 512:(ic + 1) * 512],
                                 pu[:, 0:512], t_sup[:, ic:ic + 1], t_g[eslot][:],
                                 ALU.mult, ALU.mult, inc=sDVE)
                            fuse_ready[(s, ic)] = sDVE.n
                            g_free[eslot] = sDVE.n

                    dpsum = [pB[0], pB[1], pA[0], pA[1]]
                    ev_done = {}
                    issue_wd_tile(0, 0)
                    for dc in range(4):
                        for icg in range(16):
                            wdslot = icg % 2
                            if icg + 1 < 16:
                                issue_wd_tile(dc, icg + 1)
                            elif dc + 1 < 4:
                                issue_wd_tile(dc + 1, 0)
                            if dc == 3 and icg == 14 and s + 1 < NS:
                                issue_gu_block(0)
                            P.wait("tensor", SL[f"wd{wdslot}"], SL[f"wd{wdslot}"].n)
                            if icg == 0:
                                P.wait("tensor", sDVE, fuse_ready[(s, 63)])
                                for tt in range(NTT):
                                    if (dc - 1, tt) in ev_done:
                                        P.wait("tensor", sDVE, ev_done[(dc - 1, tt)])
                            for j in range(4):
                                ic = icg * 4 + j
                                for tt in range(NTT):
                                    P.op("tensor", "matmul", dpsum[tt][:, 0:512],
                                         t_fuse[:, ic * 512 + tt * 128: ic * 512 + tt * 128 + 128],
                                         t_wd[wdslot][:, j * 512:(j + 1) * 512],
                                         start=(ic == 0), stop=(ic == 63),
                                         inc=(sPE if (tt == NTT - 1 and j == 3) else None))
                            wbusy[("d", wdslot)] = (sPE, sPE.n)
                        down_mm = sPE.n
                        for tt in range(NTT):
                            P.wait("vector", sPE, down_mm)
                            dslot = tt % 2
                            t_dt = t_ab[0][:, dslot * 512:dslot * 512 + 512]
                            P.drain("vector")
                            P.op("vector", "tensor_mul", t_dt,
                                 dpsum[tt][:, 0:512], t_sdb[:, dc * 512:(dc + 1) * 512])
                            P.drain("vector")
                            P.op("vector", "tensor_add", t_xa[tt][:, dc * 512:(dc + 1) * 512],
                                 t_xa[tt][:, dc * 512:(dc + 1) * 512], t_dt,
                                 inc=sDVE)
                            ev_done[(dc, tt)] = sDVE.n
                    ab_free[0] = (sDVE, sDVE.n)
                    for tt in range(NTT):
                        gt = s * NTT + tt
                        P.wait("scalar", sDVE, ev_done[(3, tt)])
                        sx = SL[XS[tt]]
                        P.op("scalar", "dma_start", out=xout[gt * 128:(gt + 1) * 128, :],
                             in_=t_xa[tt][:], inc=sx, inck=16)
                        xa_free[tt] = (sx, sx.n)

        def barrier():
            P.wait("gpsimd", dQ, CASTS)
            for k in ("wg0", "wg1", "wu0", "wu1", "wd0", "wd1"):
                P.wait("gpsimd", SL[k], SL[k].n)
            for k in ("xa0", "xa1", "xb0", "xb1", "xat0", "xat1", "ao0", "ao1"):
                P.wait("scalar", SL[k], SL[k].n)
            P.wait("sync", dT, dT.n)
            for k in ("wo0", "wo1", "awg0", "awg1", "awu0", "awu1",
                      "awg2", "awg3", "awu2", "awu3"):
                P.wait("sync", SL[k], SL[k].n)
            target = sBAR.n + 5
            for e in Prog.ENGINES:
                if e in ("scalar", "vector", "tensor", "gpsimd"):
                    P.drain(e)
                P.op(e, "nop", inc=sBAR)
            for e in Prog.ENGINES:
                P.wait(e, sBAR, target)

        def attn_phase():
            with ExitStack() as rctx:
                RE = rctx.enter_context
                t_xat = [RE(nc.sbuf_tensor(f"a_x{i}", [64, D], F32)) for i in range(2)]
                t_vh = [RE(nc.sbuf_tensor(f"a_vh{i}", [64, D], BF16)) for i in range(8)]
                t_sq6 = RE(nc.sbuf_tensor("a_sq6", [64, D], BF16))
                t_oT = RE(nc.sbuf_tensor("a_oT", [128, NHC * STRIPE], BF16))
                t_z = RE(nc.sbuf_tensor("a_z", [128, NHC * STRIPE], BF16))

                t_sgt = [RE(nc.sbuf_tensor(f"a_sg{i}", [128, 512], BF16)) for i in range(2)]
                t_aoe = [RE(nc.sbuf_tensor(f"a_ao{i}", [128, 512], F32)) for i in range(2)]
                t_zt = t_aoe[0]
                t_wot = [RE(nc.sbuf_tensor(f"a_wo{i}", [128, 512], BF16)) for i in range(2)]
                t_qz = [RE(nc.sbuf_tensor(f"a_qz{i}", [64, 128], F32)) for i in range(2)]
                t_qb = [RE(nc.sbuf_tensor(f"a_qb{i}", [64, 128], BF16)) for i in range(2)]
                t_kf = [RE(nc.sbuf_tensor(f"a_kf{i}", [64, 128], F32)) for i in range(2)]
                t_kb = [RE(nc.sbuf_tensor(f"a_kb{i}", [64, 128], BF16)) for i in range(2)]
                t_e1 = [RE(nc.sbuf_tensor(f"a_e1{i}", [64, 128], F32)) for i in range(2)]
                t_sp = [RE(nc.sbuf_tensor(f"a_sp{i}", [64, 128], F32)) for i in range(2)]
                t_nb = [RE(nc.sbuf_tensor(f"a_nb{i}", [64, 128], F32)) for i in range(2)]
                t_kzf = [RE(nc.sbuf_tensor(f"a_kzf{i}", [64, 128], F32)) for i in range(2)]
                t_nbT = [RE(nc.sbuf_tensor(f"a_nbT{i}", [128, 64], F32)) for i in range(2)]
                t_eB = [RE(nc.sbuf_tensor(f"a_eB{i}", [128, 64], F32)) for i in range(2)]
                t_emB = [RE(nc.sbuf_tensor(f"a_emB{i}", [128, 64], F32)) for i in range(2)]
                t_ed = [RE(nc.sbuf_tensor(f"a_ed{i}", [128, 64], F32)) for i in range(2)]
                t_eld = [RE(nc.sbuf_tensor(f"a_eld{i}", [128, 1], F32)) for i in range(8)]
                t_qT = [RE(nc.sbuf_tensor(f"a_qT{i}", [128, 64], BF16)) for i in range(2)]
                t_kT = [RE(nc.sbuf_tensor(f"a_kT{i}", [128, 64], BF16)) for i in range(2)]
                t_qeB = [RE(nc.sbuf_tensor(f"a_qeB{i}", [128, 64], BF16)) for i in range(8)]
                t_kme = [RE(nc.sbuf_tensor(f"a_kme{i}", [128, 64], BF16)) for i in range(2)]
                t_kdT = [RE(nc.sbuf_tensor(f"a_kdT{i}", [128, 64], BF16)) for i in range(2)]
                t_kd = [RE(nc.sbuf_tensor(f"a_kd{i}", [64, 128], BF16)) for i in range(8)]
                t_AT = [RE(nc.sbuf_tensor(f"a_AT{i}", [64, 64], BF16)) for i in range(8)]
                t_dg = [RE(nc.sbuf_tensor(f"a_dg{i}", [128, 128], BF16)) for i in range(8)]
                t_A2 = [RE(nc.sbuf_tensor(f"a_A2{i}", [64, 64], BF16)) for i in range(4)]
                t_qe2 = [RE(nc.sbuf_tensor(f"a_qe2{i}", [128, 64], BF16)) for i in range(4)]
                t_sm = RE(nc.sbuf_tensor("a_sm", [64, 24], F32))
                t_yT2 = RE(nc.sbuf_tensor("a_yT2", [128, NHC * STRIPE], BF16))
                t_rr = t_aoe[0]
                t_rsb = t_aoe[1]
                yT_of = lambda s: (t_yT if s % 2 == 0 else t_yT2)

                P.op("vector", "memset", t_Sb[:], 0.0, inc=sDVE)
                x_free = [None, None]
                vh_free = [None] * 8
                slot_guard = [None] * 8
                wt_busy = {}
                st_q = {"q_act": 0, "k_act": 0, "sup": [0, 0, 0, 0], "sb": 0, "oT_ev": 0,
                        "epi_act": 0, "pf": 0}
                pg_guard = {0: {}, 1: {}}
                prep_out = {}

                def load_x_chunk(gch):
                    slot = gch % 2
                    sl = SL[f"xat{slot}"]
                    if x_free[slot] is not None:
                        P.wait("scalar", x_free[slot][0], x_free[slot][1])
                    P.op("scalar", "dma_start", out=t_xat[slot][:],
                         in_=x1s[gch * 64:(gch + 1) * 64, :], inc=sl, inck=16)
                    return (sl, sl.n)

                vh_tr_of = {}

                def emit_prep_stages(gch):
                    xld = load_x_chunk(gch)
                    p = gch % 2
                    s, ch = divmod(gch, NCH)
                    G = pg_guard[p]
                    st = p * 8
                    P.wait("vector", xld[0], xld[1])
                    P.wait("scalar", xld[0], xld[1])
                    if vh_free[ch] is not None:
                        P.wait("scalar", vh_free[ch][0], vh_free[ch][1])
                    if gch > 0:
                        P.wait("scalar", sPE, vh_tr_of[gch - 1])
                    emit_ln(t_xat[p][:], t_vh[ch][:], t_sq6[0:64, :], t_sm, st, 64)
                    vh_rdy = sACT.n
                    x_free[p] = (sACT, vh_rdy)
                    yield
                    P.wait("tensor", sACT, vh_rdy)
                    yTs = yT_of(s)
                    transpose_128(
                        t_vh[ch],
                        lambda hcc, ch=ch, yTs=yTs: yTs[:, hcc * STRIPE + ch * 64: hcc * STRIPE + ch * 64 + 64],
                        NHC, t_i64b[:], 64)
                    vh_tr = sPE.n
                    vh_tr_of[gch] = vh_tr
                    P.wait("tensor", sDVE, max(st_q["q_act"], st_q["k_act"]))
                    for hc in range(NHC):
                        P.op("tensor", "matmul", pA[0][0:64, 0:128],
                             yTs[:, hc * STRIPE + ch * 64: hc * STRIPE + ch * 64 + 64],
                             t_wqT[:, hc * DK:(hc + 1) * DK],
                             start=(hc == 0), stop=(hc == NHC - 1),
                             inc=(sPE if hc == NHC - 1 else None))
                        P.op("tensor", "matmul", pA[1][0:64, 0:128],
                             yTs[:, hc * STRIPE + ch * 64: hc * STRIPE + ch * 64 + 64],
                             t_wkT[:, hc * DK:(hc + 1) * DK],
                             start=(hc == 0), stop=(hc == NHC - 1),
                             inc=(sPE if hc == NHC - 1 else None))
                    kz_mm = sPE.n
                    qz_mm = kz_mm - 1
                    P.wait("vector", sPE, kz_mm)
                    if "qzf_r" in G:
                        P.wait("vector", sACT, G["qzf_r"])
                    P.op("vector", "tensor_copy", t_nb[p][:], pA[0][0:64, 0:128])
                    P.op("vector", "tensor_copy", t_kzf[p][:], pA[1][0:64, 0:128], inc=sDVE)
                    st_q["q_act"] = sDVE.n
                    st_q["k_act"] = sDVE.n
                    qkz_ev = sDVE.n
                    P.wait("scalar", sDVE, qkz_ev)
                    if "qz_w" in G:
                        P.wait("scalar", G["qz_w"][0], G["qz_w"][1])
                    P.op("scalar", "activation", t_qz[p][:], t_nb[p][:], AF.Exp,
                         accum_out=t_sm[:, 17:18], inc=sACT)
                    G["qzf_r"] = sACT.n
                    P.wait("vector", sACT, sACT.n)
                    P.op("vector", "reciprocal", t_sm[:, 18:19], t_sm[:, 17:18])
                    P.drain("vector")
                    if "qb_r" in G:
                        P.wait("vector", sPE, G["qb_r"])
                    P.op("vector", "tensor_scalar_mul", t_qb[p][:], t_qz[p][:], t_sm[:, 18:19],
                         inc=sDVE)
                    q_rdy = (sDVE, sDVE.n)
                    G["qz_w"] = q_rdy
                    if "e1_w" in G:
                        P.wait("scalar", sDVE, G["e1_w"])
                    P.op("scalar", "activation", t_e1[p][:], t_kzf[p][:], AF.Exp,
                         scale=-1.0, inc=sACT)
                    P.wait("vector", sACT, sACT.n)
                    if "sp_r" in G:
                        P.wait("vector", G["sp_r"][0], G["sp_r"][1])
                    P.drain("vector")
                    P.op("vector", "tensor_scalar_add", t_kf[p][:], t_e1[p][:], 1.0)
                    P.drain("vector")
                    P.op("vector", "reciprocal", t_kf[p][:], t_kf[p][:], inc=sDVE)
                    G["e1_w"] = sDVE.n
                    k_rdy = sDVE.n
                    P.wait("gpsimd", sDVE, k_rdy)
                    if "qb_r" in G:
                        P.wait("gpsimd", sPE, G["qb_r"])
                    P.drain("gpsimd")
                    P.op("gpsimd", "tensor_copy", t_kb[p][:], t_kf[p][:], inc=sPL)
                    kb_rdy = (sPL, sPL.n)
                    P.wait("scalar", sDVE, k_rdy)
                    P.op("scalar", "activation", t_e1[p][:], t_kf[p][:], AF.Exp, scale=-1.0)
                    P.drain("scalar")
                    P.op("scalar", "activation", t_e1[p][:], t_e1[p][:], AF.Ln,
                         bias=t_cm[0:64, 1:2], inc=sACT)
                    P.wait("gpsimd", sACT, sACT.n)
                    if "sp_mm" in G:
                        P.wait("gpsimd", sPE, G["sp_mm"])
                    P.drain("gpsimd")
                    P.op("gpsimd", "tensor_add", t_sp[p][:], t_kf[p][:], t_e1[p][:], inc=sPL)
                    sp_rdy = (sPL, sPL.n)
                    G["sp_r"] = sp_rdy
                    yield
                    P.wait("tensor", sp_rdy[0], sp_rdy[1])
                    P.wait("tensor", sDVE, st_q["pf"])
                    P.op("tensor", "matmul", pF[:, 128:192], t_sp[p][:], t_triu[:],
                         start=True, stop=True, inc=sPE)
                    G["sp_mm"] = sPE.n
                    P.wait("vector", sPE, sPE.n)
                    if "ed_r" in G:
                        P.wait("vector", sACT, G["ed_r"])
                    P.op("vector", "tensor_copy", t_nbT[p][:], pF[:, 128:192], inc=sDVE)
                    nbT_rdy = sDVE.n
                    st_q["pf"] = nbT_rdy
                    P.wait("scalar", sDVE, nbT_rdy)
                    if "sc_r" in G:
                        P.wait("scalar", G["sc_r"][0], G["sc_r"][1])
                    P.op("scalar", "activation", t_eB[p][:], t_nbT[p][:], AF.Exp, scale=-1.0)
                    P.drain("scalar")
                    P.op("scalar", "activation", t_emB[p][:], t_nbT[p][:], AF.Exp, inc=sACT)
                    exps_rdy = sACT.n
                    P.wait("vector", sDVE, nbT_rdy)
                    P.drain("vector")
                    P.op("vector", "tensor_scalar", t_ed[p][:], t_nbT[p][:], t_nbT[p][:, 63:64],
                         None, ALU.subtract, inc=sDVE)
                    P.wait("scalar", sDVE, sDVE.n)
                    if slot_guard[ch] is not None:
                        P.wait("scalar", sDVE, slot_guard[ch]["stt"])
                    P.drain("scalar")
                    P.op("scalar", "activation", t_ed[p][:], t_ed[p][:], AF.Exp)
                    P.drain("scalar")
                    P.op("scalar", "activation", t_eld[ch][:], t_nbT[p][:, 63:64], AF.Exp,
                         scale=-1.0, inc=sACT)
                    ed_rdy = sACT.n
                    G["ed_r"] = ed_rdy
                    P.wait("vector", sACT, ed_rdy)
                    if slot_guard[ch] is not None:
                        P.wait("vector", sPE, slot_guard[ch]["mm"])
                    P.drain("vector")
                    P.op("vector", "tensor_scalar_mul", t_dg[ch][:], t_i128b[:],
                         t_eld[ch][:, 0:1], inc=sDVE)
                    dg_rdy = sDVE.n
                    P.wait("tensor", q_rdy[0], q_rdy[1])
                    P.wait("tensor", kb_rdy[0], kb_rdy[1])
                    P.wait("tensor", sDVE, ptfree[0])
                    P.op("tensor", "transpose", pT[:, 0:64], t_qb[p][:], t_i64b[:])
                    P.op("tensor", "transpose", pT[:, 64:128], t_kb[p][:], t_i64b[:], inc=sPE)
                    G["qb_r"] = sPE.n
                    P.wait("vector", sPE, sPE.n)
                    if "qT_r" in G:
                        P.wait("vector", G["qT_r"][0], G["qT_r"][1])
                    P.op("vector", "tensor_copy", t_qT[p][:], pT[:, 0:64])
                    P.op("vector", "tensor_copy", t_kT[p][:], pT[:, 64:128], inc=sDVE)
                    qkT_rdy = sDVE.n
                    ptfree[0] = qkT_rdy
                    P.wait("tensor", sDVE, qkT_rdy)
                    P.wait("gpsimd", sDVE, qkT_rdy)
                    P.wait("gpsimd", sACT, exps_rdy)
                    if slot_guard[ch] is not None:
                        P.wait("gpsimd", sPE, slot_guard[ch]["mm"])
                    P.drain("gpsimd")
                    P.op("gpsimd", "tensor_mul", t_qeB[ch][:], t_qT[p][:], t_eB[p][:])
                    P.drain("gpsimd")
                    P.op("gpsimd", "tensor_mul", t_kme[p][:], t_kT[p][:], t_emB[p][:], inc=sPL)
                    P.wait("gpsimd", sACT, ed_rdy)
                    P.wait("gpsimd", sPE, st_q.get("a2_mm", 0))
                    P.drain("gpsimd")
                    P.op("gpsimd", "tensor_mul", t_kdT[p][:], t_kT[p][:], t_ed[p][:], inc=sPL)
                    scaled_rdy = (sPL, sPL.n)
                    G["sc_r"] = scaled_rdy
                    G["qT_r"] = scaled_rdy
                    yield
                    P.wait("tensor", scaled_rdy[0], scaled_rdy[1])
                    P.wait("tensor", sDVE, st_q["pf"])
                    P.op("tensor", "matmul", pF[0:64, 192:256], t_kme[p][:], t_qeB[ch][:],
                         start=True, stop=True, inc=sPE)
                    at_mm = sPE.n
                    P.op("tensor", "transpose", pT[0:64, 0:128], t_kdT[p][:], t_i128b[:], inc=sPE)
                    kd_mm = sPE.n
                    P.wait("vector", sPE, at_mm)
                    if slot_guard[ch] is not None:
                        P.wait("vector", sPE, slot_guard[ch]["mm"])
                    P.op("vector", "tensor_mul", t_AT[ch][:], pF[0:64, 192:256], t_triu[:],
                         inc=sDVE)
                    at_rdy = sDVE.n
                    st_q["pf"] = at_rdy
                    P.wait("vector", sPE, kd_mm)
                    if slot_guard[ch] is not None:
                        P.wait("vector", sPE, slot_guard[ch]["mm"])
                    P.op("vector", "tensor_copy", t_kd[ch][:], pT[0:64, 0:128], inc=sDVE)
                    kd_rdy = sDVE.n
                    ptfree[0] = kd_rdy
                    a2_rdy = 0
                    if ch % 2 == 1:
                        p2 = (ch - 1) % 2
                        P.wait("tensor", sDVE, st_q["pf"])
                        P.op("tensor", "matmul", pF[0:64, 256:320], t_kdT[p2][:], t_qeB[ch][:],
                             start=True, stop=True, inc=sPE)
                        a2_mm = sPE.n
                        st_q["a2_mm"] = a2_mm
                        P.wait("vector", sPE, a2_mm)
                        if slot_guard[ch] is not None:
                            P.wait("vector", sPE, slot_guard[ch]["mm"])
                        P.op("vector", "tensor_copy", t_A2[ch // 2][:], pF[0:64, 256:320], inc=sDVE)
                        st_q["pf"] = sDVE.n
                        P.drain("vector")
                        P.op("vector", "tensor_scalar_mul", t_qe2[ch // 2][:], t_qeB[ch][:],
                             t_eld[ch - 1][:, 0:1], inc=sDVE)
                        a2_rdy = sDVE.n
                    prep_out[gch] = {"vh_tr": vh_tr, "at_rdy": at_rdy, "kd_rdy": kd_rdy,
                                     "ed_rdy": ed_rdy, "a2_rdy": a2_rdy, "dg_rdy": dg_rdy}

                def emit_body_pair(gch):
                    s, ch0 = divmod(gch, NCH)
                    ch1 = ch0 + 1
                    po0 = prep_out[gch]
                    po1 = prep_out[gch + 1]
                    P.wait("tensor", sDVE, po0["kd_rdy"])
                    P.wait("tensor", sDVE, po0["at_rdy"])
                    P.wait("tensor", sDVE, po1["kd_rdy"])
                    P.wait("tensor", sDVE, po1["at_rdy"])
                    P.wait("tensor", sDVE, po1["a2_rdy"])
                    if st_q["sb"]:
                        P.wait("tensor", sDVE, st_q["sb"])
                    if st_q["oT_ev"]:
                        P.wait("tensor", sDVE, st_q["oT_ev"])
                    for m8 in range(8):
                        for half in range(2):
                            m = m8 + 8 * half
                            P.op("tensor", "matmul", pB[half][:, m8 * 64:m8 * 64 + 64],
                                 t_vh[ch0][:, m * 128:(m + 1) * 128], t_AT[ch0][:],
                                 start=True, stop=False)
                        for half in range(2):
                            m = m8 + 8 * half
                            P.op("tensor", "matmul", pB[half][:, m8 * 64:m8 * 64 + 64],
                                 t_Sb[:, m * 128:(m + 1) * 128], t_qeB[ch0][:],
                                 start=False, stop=True,
                                 inc=(sPE if m8 == 7 else None))
                    oT0 = [sPE.n - 1, sPE.n]
                    for m8 in range(8):
                        for half in range(2):
                            m = m8 + 8 * half
                            P.op("tensor", "matmul", pA[2 + half][:, m8 * 64:m8 * 64 + 64],
                                 t_vh[ch1][:, m * 128:(m + 1) * 128], t_AT[ch1][:],
                                 start=True, stop=False)
                        for half in range(2):
                            m = m8 + 8 * half
                            P.op("tensor", "matmul", pA[2 + half][:, m8 * 64:m8 * 64 + 64],
                                 t_vh[ch0][:, m * 128:(m + 1) * 128], t_A2[ch1 // 2][:],
                                 start=False, stop=False)
                        for half in range(2):
                            m = m8 + 8 * half
                            P.op("tensor", "matmul", pA[2 + half][:, m8 * 64:m8 * 64 + 64],
                                 t_Sb[:, m * 128:(m + 1) * 128], t_qe2[ch1 // 2][:],
                                 start=False, stop=True,
                                 inc=(sPE if m8 == 7 else None))
                    oT1 = [sPE.n - 1, sPE.n]
                    if ch0 == 0 and st_q["epi_act"]:
                        P.wait("vector", sACT, st_q["epi_act"])
                    for half in range(2):
                        P.wait("vector", sPE, oT0[half])
                        P.op("vector", "tensor_copy",
                             t_oT[:].rearrange("p (m t) -> p m t", m=NHC)[:, half * 8:half * 8 + 8,
                                                                          ch0 * 64:(ch0 + 1) * 64],
                             pB[half][:].rearrange("p (m t) -> p m t", m=8), inc=sDVE)
                    ev0 = sDVE.n
                    for half in range(2):
                        P.wait("vector", sPE, oT1[half])
                        P.op("vector", "tensor_copy",
                             t_oT[:].rearrange("p (m t) -> p m t", m=NHC)[:, half * 8:half * 8 + 8,
                                                                          ch1 * 64:(ch1 + 1) * 64],
                             pA[2 + half][:].rearrange("p (m t) -> p m t", m=8), inc=sDVE)
                    ev1 = sDVE.n
                    st_q["oT_ev"] = ev1
                    kv0_banks = [pA[0], pA[1], pF, pB[0]]
                    kv1_banks = [pB[1], pA[2], pA[3], pA[0]]
                    P.wait("tensor", sDVE, st_q["q_act"])
                    P.wait("tensor", sDVE, st_q["pf"])
                    P.wait("tensor", sDVE, ev0)
                    P.wait("tensor", sDVE, po0["dg_rdy"])
                    for n in range(4):
                        P.op("tensor", "matmul", kv0_banks[n][:, 0:512], t_dg[ch0][:],
                             t_Sb[:, n * 512:(n + 1) * 512], start=True, stop=False)
                    for n in range(4):
                        P.op("tensor", "matmul", kv0_banks[n][:, 0:512], t_kd[ch0][:],
                             t_vh[ch0][:, n * 512:(n + 1) * 512], start=False, stop=True,
                             inc=(sPE if n == 3 else None))
                    kv0 = sPE.n
                    P.wait("vector", sPE, kv0)
                    P.drain("vector")
                    s_up0 = []
                    for n in range(4):
                        P.op("vector", "tensor_copy", t_Sb[:, n * 512:(n + 1) * 512],
                             kv0_banks[n][:, 0:512], inc=sDVE)
                        s_up0.append(sDVE.n)
                    P.wait("tensor", sDVE, s_up0[3])
                    P.wait("tensor", sDVE, ev1)
                    P.wait("tensor", sDVE, po1["dg_rdy"])
                    for n in range(4):
                        P.op("tensor", "matmul", kv1_banks[n][:, 0:512], t_dg[ch1][:],
                             t_Sb[:, n * 512:(n + 1) * 512], start=True, stop=False)
                    for n in range(4):
                        P.op("tensor", "matmul", kv1_banks[n][:, 0:512], t_kd[ch1][:],
                             t_vh[ch1][:, n * 512:(n + 1) * 512], start=False, stop=True,
                             inc=(sPE if n == 3 else None))
                    kv1b = sPE.n
                    P.wait("vector", sPE, kv1b)
                    P.drain("vector")
                    s_up1 = []
                    for n in range(4):
                        P.op("vector", "tensor_copy", t_Sb[:, n * 512:(n + 1) * 512],
                             kv1_banks[n][:, 0:512], inc=sDVE)
                        s_up1.append(sDVE.n)
                    st_q["sup"] = [s_up1[3]] * 4
                    st_q["sb"] = s_up1[3]
                    st_q["pf"] = s_up1[3]
                    st_q["q_act"] = s_up1[3]
                    st_q["k_act"] = s_up1[3]
                    vh_free[ch0] = (sDVE, s_up1[3])
                    vh_free[ch1] = (sDVE, s_up1[3])
                    slot_guard[ch0] = {"mm": kv1b, "stt": s_up1[3]}
                    slot_guard[ch1] = {"mm": kv1b, "stt": s_up1[3]}

                epi = {}

                def emit_rms(e):
                    sq_prev = 0
                    P.wait("vector", sDVE, st_q["oT_ev"])
                    P.wait("gpsimd", sDVE, st_q["oT_ev"])
                    P.wait("tensor", sDVE, sDVE.n)
                    mm_done = {}
                    for mc in range(NHC):
                        eng, sem = (("vector", sDVE) if mc % 2 == 0 else ("gpsimd", sPL))
                        if mc >= 2:
                            P.wait(eng, sPE, mm_done[mc - 2])
                        P.drain(eng)
                        P.op(eng, "tensor_mul", t_sgt[mc % 2][:],
                             t_oT[:, mc * STRIPE:(mc + 1) * STRIPE],
                             t_oT[:, mc * STRIPE:(mc + 1) * STRIPE], inc=sem)
                        P.wait("tensor", sem, sem.n)
                        if mc == 0:
                            P.wait("tensor", sDVE, st_q["sup"][3])
                        P.op("tensor", "matmul", pA[2][0:1, 0:512],
                             t_ones[:, 0:1], t_sgt[mc % 2][:],
                             start=(mc == 0), stop=(mc == NHC - 1), inc=sPE)
                        mm_done[mc] = sPE.n
                    rms_mm = sPE.n
                    P.wait("vector", sPE, rms_mm)
                    P.wait("vector", SL["ao0"], SL["ao0"].n)
                    P.op("vector", "tensor_scalar_mul", t_rr[0:1, 0:512], pA[2][0:1, 0:512], 1.0 / D,
                         inc=sDVE)
                    rr_ev = sDVE.n
                    P.wait("scalar", sDVE, rr_ev)
                    P.op("scalar", "activation", t_rr[0:1, 0:512], t_rr[0:1, 0:512], AF.Ln,
                         bias=t_cm[0:1, 0:1])
                    P.drain("scalar")
                    P.op("scalar", "activation", t_rr[0:1, 0:512], t_rr[0:1, 0:512], AF.Exp,
                         scale=-0.5, inc=sACT)
                    P.wait("vector", sACT, sACT.n)
                    P.drain("vector")
                    P.op("vector", "tensor_copy", t_sgt[0][0:1, 0:512], t_rr[0:1, 0:512], inc=sDVE)
                    P.wait("tensor", sDVE, sDVE.n)
                    P.op("tensor", "matmul", pA[2][:, 0:512], t_ones[0:1, :], t_sgt[0][0:1, 0:512],
                         start=True, stop=True, inc=sPE)
                    P.wait("vector", sPE, sPE.n)
                    P.wait("vector", SL["ao1"], SL["ao1"].n)
                    P.op("vector", "tensor_copy", t_rsb[:], pA[2][:, 0:512], inc=sDVE)
                    epi["rsb"] = sDVE.n
                    epi["go_busy"] = {}
                    epi["z_done"] = {}

                def issue_go_pair(mcp):
                    half = mcp % 2
                    for wname, wsrc in (("g", wogT), ("u", wigT)):
                        for wslot in (0, 1):
                            mc = 2 * mcp + wslot
                            key = (wname, wslot, half)
                            if key in wt_busy:
                                P.wait("sync", wt_busy[key][0], wt_busy[key][1])
                            buf = (t_wg if wname == "g" else t_wu)[wslot]
                            sl = SL[("awg" if wname == "g" else "awu") + str(wslot + 2 * half)]
                            P.op("sync", "dma_start",
                                 out=buf[:, half * 2048:half * 2048 + 2048],
                                 in_=wsrc[:, mc * 2048:(mc + 1) * 2048],
                                 inc=sl, inck=16)

                def emit_goT_pair(e, mcp):
                    go_busy = epi["go_busy"]
                    z_done = epi["z_done"]
                    half = mcp % 2
                    co = half * 2048
                    mcs = (2 * mcp, 2 * mcp + 1)
                    if mcp == 0:
                        issue_go_pair(0)
                    if mcp + 1 < NHC // 2:
                        issue_go_pair(mcp + 1)
                    for wslot in (0, 1):
                        for nmw in ("awg", "awu"):
                            sl = SL[nmw + str(wslot + 2 * half)]
                            P.wait("tensor", sl, sl.n)
                    gbank = (pA[2], pA[3]) if mcp % 2 == 0 else (pB[0], pB[1])
                    if mcp == 0:
                        P.wait("tensor", sDVE, epi["rsb"])
                        P.wait("tensor", sDVE, st_q["sup"][3])
                    elif mcp == 1:
                        P.wait("tensor", sDVE, st_q["sup"][3])
                    else:
                        P.wait("tensor", sACT, go_busy[2 * mcp - 4])
                        P.wait("tensor", sACT, go_busy[2 * mcp - 3])
                    for hc in range(NHC):
                        for wslot in (0, 1):
                            P.op("tensor", "matmul", gbank[wslot][:, 0:512],
                                 t_wg[wslot][:, co + hc * 128: co + (hc + 1) * 128],
                                 t_oT[:, hc * STRIPE:(hc + 1) * STRIPE],
                                 start=(hc == 0), stop=False)
                    for hc in range(NHC):
                        for wslot in (0, 1):
                            P.op("tensor", "matmul", gbank[wslot][:, 0:512],
                                 t_wu[wslot][:, co + hc * 128: co + (hc + 1) * 128],
                                 yT_of(e)[:, hc * STRIPE:(hc + 1) * STRIPE],
                                 start=False, stop=(hc == NHC - 1),
                                 inc=(sPE if hc == NHC - 1 else None))
                    go_mm = sPE.n
                    for wname in ("g", "u"):
                        for wslot in (0, 1):
                            wt_busy[(wname, wslot, half)] = (sPE, go_mm)
                    for wslot in (0, 1):
                        mc = mcs[wslot]
                        P.wait("scalar", sPE, go_mm)
                        if mc >= 2:
                            P.wait("scalar", sDVE, z_done[mc - 2])
                        P.op("scalar", "activation", t_sgt[wslot][:], gbank[wslot][:, 0:512],
                             AF.Silu, inc=sACT)
                        go_busy[mc] = sACT.n
                        P.wait("vector", sACT, go_busy[mc])
                        P.wait("vector", sDVE, epi["rsb"])
                        P.wait("vector", SL["ao0"], SL["ao0"].n)
                        P.drain("vector")
                        P.op("vector", "scalar_tensor_tensor", t_zt[:],
                             t_oT[:, mc * STRIPE:(mc + 1) * STRIPE], t_gnw[:, mc:mc + 1], t_rsb[:],
                             ALU.mult, ALU.mult)
                        P.drain("vector")
                        P.op("vector", "tensor_mul", t_z[:, mc * STRIPE:(mc + 1) * STRIPE],
                             t_zt[:], t_sgt[wslot][:], inc=sDVE)
                        z_done[mc] = sDVE.n
                    if mcp == NHC // 2 - 1:
                        epi["zT_all"] = sDVE.n
                        st_q["epi_act"] = sACT.n

                def emit_ao(e, prep_cb=None):
                    go_busy = epi["go_busy"]
                    P.wait("tensor", sDVE, epi["zT_all"])
                    ao_psums = [pB[0], pB[1], pA[2], pA[3]]
                    ao_ev_done = 0
                    for mn in range(4):
                        if mn == 0:
                            P.wait("tensor", sDVE, max(st_q["oT_ev"], st_q["sup"][3]))
                            P.wait("tensor", sACT, go_busy[NHC - 1])
                        else:
                            P.wait("tensor", sDVE, ao_ev_done)
                        for dvc in range(NHC):
                            wslot2 = dvc % 2
                            key = ("wo", wslot2)
                            if key in wt_busy:
                                P.wait("sync", wt_busy[key][0], wt_busy[key][1])
                            sl = SL[f"wo{wslot2}"]
                            P.op("sync", "dma_start", out=t_wot[wslot2][:, 0:512],
                                 in_=woT[dvc * 128:(dvc + 1) * 128, mn * 512:(mn + 1) * 512],
                                 inc=sl, inck=16)
                            P.wait("tensor", sl, sl.n)
                            for tt in range(NTT):
                                P.op("tensor", "matmul", ao_psums[tt][:, 0:512],
                                     t_z[:, dvc * STRIPE + tt * 128: dvc * STRIPE + tt * 128 + 128],
                                     t_wot[wslot2][:, 0:512],
                                     start=(dvc == 0), stop=(dvc == NHC - 1),
                                     inc=(sPE if tt == NTT - 1 else None))
                            wt_busy[key] = (sPE, sPE.n)
                        ao_mm = sPE.n
                        for tt in range(NTT):
                            aslot = tt % 2
                            sl = SL[f"ao{aslot}"]
                            P.wait("vector", sl, sl.n)
                            P.wait("vector", sPE, ao_mm)
                            P.drain("vector")
                            P.op("vector", "tensor_copy", t_aoe[aslot][:], ao_psums[tt][:, 0:512],
                                 inc=sDVE)
                            P.wait("scalar", sDVE, sDVE.n)
                            P.op("scalar", "dma_start",
                                 out=aos[(e * NTT + tt) * 128:(e * NTT + tt + 1) * 128,
                                         mn * 512:(mn + 1) * 512],
                                 in_=t_aoe[aslot][:], inc=sl, inck=16)
                        ao_ev_done = sDVE.n
                        if prep_cb is not None:
                            prep_cb(mn)
                    st_q["sup"] = [ao_ev_done] * 4
                    st_q["oT_ev"] = ao_ev_done

                def pump(gens):
                    alive = []
                    for g in gens:
                        try:
                            next(g)
                            alive.append(g)
                        except StopIteration:
                            pass
                    gens[:] = alive

                gens = []
                for ch in range(NCH):
                    pump(gens)
                    g = emit_prep_stages(ch)
                    next(g)
                    gens.append(g)
                while gens:
                    pump(gens)
                for ch in range(0, NCH, 2):
                    emit_body_pair(ch)
                for s in range(1, NS):
                    emit_rms(s - 1)
                    gens = []
                    nxt = [0]

                    def step(s=s, gens=gens, nxt=nxt):
                        pump(gens)
                        if nxt[0] < NCH:
                            g = emit_prep_stages(s * NCH + nxt[0])
                            nxt[0] += 1
                            next(g)
                            gens.append(g)
                    step()
                    for k in range(NHC // 2):
                        emit_goT_pair(s - 1, k)
                        step()
                    emit_ao(s - 1, prep_cb=lambda mn, f=step: f())
                    while gens:
                        pump(gens)
                    for ch in range(0, NCH, 2):
                        emit_body_pair(s * NCH + ch)
                emit_rms(NS - 1)
                for k in range(NHC // 2):
                    emit_goT_pair(NS - 1, k)
                emit_ao(NS - 1)

        if 1 in phases:
            ffn_phase(1)
        if 2 in phases:
            barrier()
            attn_phase()
        if 3 in phases:
            barrier()
            ffn_phase(2)
        for k in ("xa0", "xa1", "xb0", "xb1"):
            P.wait("scalar", SL[k], SL[k].n)

        block = E(nc.Block())

        @block.sync
        def _(e):
            P.replay(e, "sync")

        @block.scalar
        def _(e):
            P.replay(e, "scalar")

        @block.vector
        def _(e):
            P.replay(e, "vector")

        @block.tensor
        def _(e):
            P.replay(e, "tensor")

        @block.gpsimd
        def _(e):
            P.replay(e, "gpsimd")

    return nc


def _block_gu(w):
    return np.ascontiguousarray(
        w.reshape(NHC, 128, INTER // 256, 256).transpose(1, 2, 0, 3).reshape(128, -1))


def _block_wd(w):
    return np.ascontiguousarray(
        w.reshape(INTER // 512, 4, 128, 4, 512).transpose(2, 3, 0, 1, 4).reshape(128, -1))


def make_consts():
    return dict(
        ci64b=np.eye(64, dtype=np.float32).astype(bf16),
        ci128b=np.eye(128, dtype=np.float32).astype(bf16),
        ci64f=np.eye(64, dtype=np.float32),
        ctriu=np.triu(np.ones((64, 64), np.float32)),
        cones=np.ones((128, 128), np.float32).astype(bf16),
        cmisc=np.tile(np.array([[EPS, 1.0]], np.float32), (128, 1)),
    )


def kernel(L0_attn_hidden_states, L0_attn_Wq, L0_attn_Wk, L0_attn_Wog, L0_attn_Wig,
           L0_attn_Wo, L0_attn_g_norm_w, L0_ffn_w_gate, L0_ffn_w_up, L0_ffn_w_down,
           L0_ffn_s_gate, L0_ffn_s_up, L0_ffn_s_down):
    nc = build_program()
    shared = dict(
        wq=np.ascontiguousarray(L0_attn_Wq), wk=np.ascontiguousarray(L0_attn_Wk),
        wog=np.ascontiguousarray(L0_attn_Wog), wig=np.ascontiguousarray(L0_attn_Wig),
        wo=np.ascontiguousarray(L0_attn_Wo), gnw=np.ascontiguousarray(L0_attn_g_norm_w),
        wg8b=_block_gu(L0_ffn_w_gate), wu8b=_block_gu(L0_ffn_w_up),
        wd8b=_block_wd(L0_ffn_w_down),
        sg=np.ascontiguousarray(L0_ffn_s_gate), su=np.ascontiguousarray(L0_ffn_s_up),
        sd=np.ascontiguousarray(L0_ffn_s_down), **make_consts(),
    )
    in_maps = [dict(x0=np.ascontiguousarray(L0_attn_hidden_states[c]), **shared)
               for c in range(8)]
    res = run_bass_kernel_spmd(nc, in_maps, core_ids=list(range(8)))
    return np.stack([res.results[c]["out"] for c in range(8)], axis=0)


# version 27
# speedup vs baseline: 1.0174x; 1.0020x over previous
import numpy as np
import ml_dtypes
from contextlib import ExitStack
import concourse.bass as bass
import concourse.mybir as mybir
from concourse.bass_utils import run_bass_kernel_spmd

F32 = mybir.dt.float32
BF16 = mybir.dt.bfloat16
I8 = mybir.dt.int8
AF = mybir.ActivationFunctionType
ALU = mybir.AluOpType
AX = mybir.AxisListType
bf16 = ml_dtypes.bfloat16

T = 2048
D = 2048
INTER = 8192
STRIPE = 512
NS = T // STRIPE
NTT = STRIPE // 128
C = 64
NCH = STRIPE // C
DK = 128
EPS = 1e-5
NHC = D // 128


class Sem:
    def __init__(self, nc, ctx, name):
        self.h = ctx.enter_context(nc.semaphore(name))
        self.n = 0


class Prog:
    ENGINES = ("sync", "scalar", "vector", "tensor", "gpsimd")

    def __init__(self):
        self.q = {e: [] for e in self.ENGINES}

    def op(self, eng, name, *args, inc=None, inck=1, **kw):
        if inc is not None:
            inc.n += inck
        self.q[eng].append(("op", name, args, kw, (inc.h if inc else None), inck))
        return inc.n if inc else None

    def wait(self, eng, sem, val):
        self.q[eng].append(("wait", sem.h, val))

    def drain(self, eng):
        self.q[eng].append(("drain",))

    def replay(self, eng_obj, name):
        for item in self.q[name]:
            if item[0] == "wait":
                eng_obj.wait_ge(item[1], item[2])
            elif item[0] == "drain":
                eng_obj.drain()
            else:
                _, opname, args, kw, semh, inck = item
                ins = getattr(eng_obj, opname)(*args, **kw)
                if semh is not None:
                    ins.then_inc(semh, inck)


def build_program(phases=(1, 2, 3)):
    nc = bass.Bass("TRN2", target_bir_lowering=False)

    x0 = nc.dram_tensor("x0", [T, D], F32, kind="ExternalInput").ap()
    wq = nc.dram_tensor("wq", [DK, D], F32, kind="ExternalInput").ap()
    wk = nc.dram_tensor("wk", [DK, D], F32, kind="ExternalInput").ap()
    wog = nc.dram_tensor("wog", [D, D], F32, kind="ExternalInput").ap()
    wig = nc.dram_tensor("wig", [D, D], F32, kind="ExternalInput").ap()
    wo = nc.dram_tensor("wo", [D, D], F32, kind="ExternalInput").ap()
    gnw = nc.dram_tensor("gnw", [D], F32, kind="ExternalInput").ap()
    wg8 = nc.dram_tensor("wg8b", [128, (INTER // 256) * NHC * 256], I8, kind="ExternalInput").ap()
    wu8 = nc.dram_tensor("wu8b", [128, (INTER // 256) * NHC * 256], I8, kind="ExternalInput").ap()
    wd8 = nc.dram_tensor("wd8b", [128, (INTER // 256) * 2 * D], I8, kind="ExternalInput").ap()
    sg = nc.dram_tensor("sg", [INTER], F32, kind="ExternalInput").ap()
    su = nc.dram_tensor("su", [INTER], F32, kind="ExternalInput").ap()
    sd = nc.dram_tensor("sd", [D], F32, kind="ExternalInput").ap()
    ci64b = nc.dram_tensor("ci64b", [64, 64], BF16, kind="ExternalInput").ap()
    ci128b = nc.dram_tensor("ci128b", [128, 128], BF16, kind="ExternalInput").ap()
    ci64f = nc.dram_tensor("ci64f", [64, 64], F32, kind="ExternalInput").ap()
    ctriu = nc.dram_tensor("ctriu", [64, 64], F32, kind="ExternalInput").ap()
    cones = nc.dram_tensor("cones", [128, 128], BF16, kind="ExternalInput").ap()
    cmisc = nc.dram_tensor("cmisc", [128, 2], F32, kind="ExternalInput").ap()
    out = nc.dram_tensor("out", [T, D], F32, kind="ExternalOutput").ap()
    x1s = nc.dram_tensor("x1s", [T, D], F32, kind="Internal").ap()
    aos = nc.dram_tensor("aos", [T, D], F32, kind="Internal").ap()
    wogb = nc.dram_tensor("wogb", [D, D], BF16, kind="Internal").ap()
    wigb = nc.dram_tensor("wigb", [D, D], BF16, kind="Internal").ap()
    wob = nc.dram_tensor("wob", [D, D], BF16, kind="Internal").ap()
    wogT = nc.dram_tensor("wogT", [128, NHC * D], BF16, kind="Internal").ap()
    wigT = nc.dram_tensor("wigT", [128, NHC * D], BF16, kind="Internal").ap()
    woT = nc.dram_tensor("woT", [D, D], BF16, kind="Internal").ap()

    P = Prog()
    ctx = ExitStack()
    with ctx:
        E = ctx.enter_context
        sPE = Sem(nc, ctx, "sPE")
        sACT = Sem(nc, ctx, "sACT")
        sDVE = Sem(nc, ctx, "sDVE")
        sBAR = Sem(nc, ctx, "sBAR")
        sPL = Sem(nc, ctx, "sPL")
        dC = Sem(nc, ctx, "dC")
        dP = Sem(nc, ctx, "dP")
        dQ = Sem(nc, ctx, "dQ")
        dT = Sem(nc, ctx, "dT")
        SL = {k: Sem(nc, ctx, "d_" + k) for k in
              ("wg0", "wg1", "wu0", "wu1", "wd0", "wd1",
               "xa0", "xa1", "xb0", "xb1", "awg0", "awg1", "awu0", "awu1",
               "awg2", "awg3", "awu2", "awu3",
               "xat0", "xat1", "wo0", "wo1", "ao0", "ao1")}

        t_i64b = E(nc.sbuf_tensor("t_i64b", [64, 64], BF16))
        t_i128b = E(nc.sbuf_tensor("t_i128b", [128, 128], BF16))
        t_i64f = E(nc.sbuf_tensor("t_i64f", [64, 64], F32))
        t_triu = E(nc.sbuf_tensor("t_triu", [64, 64], F32))
        t_ones = E(nc.sbuf_tensor("t_ones", [128, 128], BF16))
        t_sgp = E(nc.sbuf_tensor("t_sgp", [128, 64], F32))
        t_sup = E(nc.sbuf_tensor("t_sup", [128, 64], F32))
        t_gnw = E(nc.sbuf_tensor("t_gnw", [128, 16], F32))
        t_sdb = E(nc.sbuf_tensor("t_sdb", [128, D], F32))
        t_wqT = E(nc.sbuf_tensor("t_wqT", [128, NHC * DK], BF16))
        t_wkT = E(nc.sbuf_tensor("t_wkT", [128, NHC * DK], BF16))
        t_S = E(nc.sbuf_tensor("t_S", [128, D], F32))
        t_Sb = E(nc.sbuf_tensor("t_Sb", [128, D], BF16))
        t_yT = E(nc.sbuf_tensor("t_yT", [128, NHC * STRIPE], BF16))
        t_wg = [E(nc.sbuf_tensor(f"t_wg{i}", [128, NHC * 256], BF16)) for i in range(2)]
        t_wu = [E(nc.sbuf_tensor(f"t_wu{i}", [128, NHC * 256], BF16)) for i in range(2)]
        t_ptr = E(nc.sbuf_tensor("t_ptr", [128, 512], BF16))
        t_stat = E(nc.sbuf_tensor("t_stat", [128, 16], F32))
        t_cm = E(nc.sbuf_tensor("t_cm", [128, 2], F32))

        pA = [E(nc.psum_tensor(f"pA{i}", [128, 512], F32)) for i in range(4)]
        pB = [E(nc.psum_tensor(f"pB{i}", [128, 512], F32)) for i in range(2)]
        pF = E(nc.psum_tensor("pF", [128, 512], F32))
        pT = E(nc.psum_tensor("pT", [128, 1024], BF16))

        pctx = ExitStack()
        t_sdr = pctx.enter_context(nc.sbuf_tensor("t_sdr", [1, D], BF16))
        t_wqk = pctx.enter_context(nc.sbuf_tensor("t_wqk", [128, D], BF16))
        P.op("scalar", "dma_start", out=t_i64b[:], in_=ci64b[:], inc=dC, inck=16)
        P.op("scalar", "dma_start", out=t_i128b[:], in_=ci128b[:], inc=dC, inck=16)
        P.op("scalar", "dma_start", out=t_i64f[:], in_=ci64f[:], inc=dC, inck=16)
        P.op("scalar", "dma_start", out=t_triu[:], in_=ctriu[:], inc=dC, inck=16)
        P.op("scalar", "dma_start", out=t_ones[:], in_=cones[:], inc=dC, inck=16)
        P.op("scalar", "dma_start", out=t_sgp[:], allow_slow_non_contiguous=True,
             in_=sg.rearrange("(c p) -> p c", p=128), inc=dC, inck=16)
        P.op("scalar", "dma_start", out=t_sup[:], allow_slow_non_contiguous=True,
             in_=su.rearrange("(c p) -> p c", p=128), inc=dC, inck=16)
        P.op("scalar", "dma_start", out=t_gnw[:], allow_slow_non_contiguous=True,
             in_=gnw.rearrange("(c p) -> p c", p=128), inc=dC, inck=16)
        P.op("scalar", "dma_start", out=t_cm[:], in_=cmisc[:], inc=dC, inck=16)
        CONSTS = dC.n

        P.op("gpsimd", "dma_start", out=t_sdr[:], in_=sd.rearrange("(one d) -> one d", one=1),
             inc=dP, inck=16)
        P.op("gpsimd", "dma_start", out=t_wqk[:], in_=wq[:], inc=dP, inck=16)
        P.wait("tensor", dC, CONSTS)
        P.wait("tensor", dP, 32)

        ptfree = [0]

        def transpose_128(src_tile, dst_colf, n_chunks, ident, rows):
            P.wait("tensor", sDVE, ptfree[0])
            for hc in range(n_chunks):
                slotp = hc % 8
                P.op("tensor", "transpose", pT[:, slotp * 128:slotp * 128 + rows],
                     src_tile[:, hc * 128:(hc + 1) * 128], ident, inc=sPE)
                if slotp == 7 or hc == n_chunks - 1:
                    nsl = slotp + 1
                    P.wait("vector", sPE, sPE.n)
                    for h2 in range(nsl):
                        hcc = hc - nsl + 1 + h2
                        P.op("vector", "tensor_copy", dst_colf(hcc),
                             pT[:, h2 * 128:h2 * 128 + rows], inc=sDVE)
                    ptfree[0] = sDVE.n
                    P.wait("tensor", sDVE, sDVE.n)

        transpose_128(t_wqk, lambda hc: t_wqT[:, hc * DK:(hc + 1) * DK], NHC, t_i128b[:], 128)
        P.wait("gpsimd", sDVE, sDVE.n)
        P.op("gpsimd", "dma_start", out=t_wqk[:], in_=wk[:], inc=dP, inck=16)
        P.wait("tensor", dP, 48)
        transpose_128(t_wqk, lambda hc: t_wkT[:, hc * DK:(hc + 1) * DK], NHC, t_i128b[:], 128)

        for dc2 in range(4):
            P.op("tensor", "matmul", pF[:, 0:512], t_ones[0:1, :], t_sdr[:, dc2 * 512:(dc2 + 1) * 512],
                 start=True, stop=True, inc=sPE)
            P.wait("vector", sPE, sPE.n)
            P.op("vector", "tensor_scalar_mul", t_sdb[:, dc2 * 512:(dc2 + 1) * 512], pF[:, 0:512], 0.5,
                 inc=sDVE)
            P.wait("tensor", sDVE, sDVE.n)

        PROLOGUE_DVE = sDVE.n
        pctx.close()
        P.op("gpsimd", "dma_start", out=wogb[:], in_=wog[:], inc=dQ, inck=16)
        P.op("gpsimd", "dma_start", out=wigb[:], in_=wig[:], inc=dQ, inck=16)
        P.op("gpsimd", "dma_start", out=wob[:], in_=wo[:], inc=dQ, inck=16)
        CASTS = dQ.n
        P.wait("sync", dQ, CASTS)
        for srcw, dst, blocked in ((wogb, wogT, True), (wigb, wigT, True), (wob, woT, False)):
            for hc in range(NHC):
                for m4 in range(4):
                    P.op("sync", "dma_start_transpose", t_ptr[:],
                         srcw[m4 * 512:(m4 + 1) * 512, hc * 128:(hc + 1) * 128], inc=dT, inck=16)
                    P.wait("sync", dT, dT.n)
                    if blocked:
                        P.op("sync", "dma_start",
                             out=dst[:].rearrange("p (mc hcb m) -> p mc hcb m", mc=NHC, hcb=NHC)
                                 [:, m4 * 4:(m4 + 1) * 4, hc, :],
                             in_=t_ptr[:].rearrange("p (mcl m) -> p mcl m", mcl=4),
                             inc=dT, inck=16)
                    else:
                        P.op("sync", "dma_start",
                             out=dst[hc * 128:(hc + 1) * 128, m4 * 512:(m4 + 1) * 512],
                             in_=t_ptr[:], inc=dT, inck=16)
                    P.wait("sync", dT, dT.n)
        PREP = dT.n

        def emit_ln(xt, yt, sqt, statt, st, rows, ve="vector", sVE=None):
            if sVE is None:
                sVE = sDVE
            P.op("vector", "reduce_sum", statt[:, st:st + 1], xt, AX.X, inc=sDVE)
            P.op("scalar", "activation", sqt, xt, AF.Square,
                 accum_out=statt[:, st + 1:st + 2], inc=sACT)
            P.wait(ve, sACT, sACT.n)
            if ve != "vector":
                P.wait(ve, sDVE, sDVE.n)
            P.drain(ve)
            P.op(ve, "tensor_scalar_mul", statt[:, st + 2:st + 3], statt[:, st:st + 1], 1.0 / D)
            P.drain(ve)
            P.op(ve, "tensor_scalar", statt[:, st + 3:st + 4], statt[:, st + 2:st + 3],
                 statt[:, st + 2:st + 3], -1.0, ALU.mult, ALU.mult)
            P.drain(ve)
            P.op(ve, "scalar_tensor_tensor", statt[:, st + 4:st + 5], statt[:, st + 1:st + 2],
                 1.0 / D, statt[:, st + 3:st + 4], ALU.mult, ALU.add, inc=sVE)
            P.wait("scalar", sVE, sVE.n)
            P.op("scalar", "activation", statt[:, st + 5:st + 6], statt[:, st + 4:st + 5], AF.Ln,
                 bias=t_cm[0:rows, 0:1])
            P.drain("scalar")
            P.op("scalar", "activation", statt[:, st + 6:st + 7], statt[:, st + 5:st + 6], AF.Exp,
                 scale=-0.5, inc=sACT)
            P.wait(ve, sACT, sACT.n)
            P.op(ve, "tensor_scalar", statt[:, st + 7:st + 8], statt[:, st + 2:st + 3],
                 statt[:, st + 6:st + 7], -1.0, ALU.mult, ALU.mult, inc=sVE)
            P.wait("scalar", sVE, sVE.n)
            P.drain("scalar")
            P.op("scalar", "activation", yt, xt, AF.Identity,
                 bias=statt[:, st + 7:st + 8], scale=statt[:, st + 6:st + 7], inc=sACT)
            return sACT.n

        def ffn_phase(phase):
            xin = x0 if phase == 1 else x1s
            xadd = None if phase == 1 else aos
            xout = x1s if phase == 1 else out
            with ExitStack() as rctx:
                RE = rctx.enter_context
                t_xa = [RE(nc.sbuf_tensor(f"f{phase}_xa{i}", [128, D], F32)) for i in range(4)]
                t_xn = ([RE(nc.sbuf_tensor(f"f{phase}_xn{i}", [128, D], F32)) for i in range(2)]
                        if phase == 1 else None)
                t_ab = [RE(nc.sbuf_tensor(f"f{phase}_ab0", [128, 512 if phase == 1 else D], F32))]
                t_fuse = RE(nc.sbuf_tensor(f"f{phase}_fuse", [128, 64 * STRIPE], BF16))
                t_g = [RE(nc.sbuf_tensor(f"f{phase}_g{i}", [128, 512], BF16)) for i in range(2)]

                t_sq = RE(nc.sbuf_tensor(f"f{phase}_sq", [128, D], BF16))
                t_wd = [RE(nc.sbuf_tensor(f"f{phase}_wd{i}", [128, 4 * 512], BF16)) for i in range(2)]

                bfree = {("A", i): ((sDVE, PROLOGUE_DVE) if phase == 1 else None)
                         for i in range(4)}
                bfree[("N", 0)] = bfree[("N", 1)] = None
                ab_free = [None]

                def key_of(s, tt):
                    if phase == 1 and tt < 2:
                        return ("A" if s % 2 == 0 else "N", tt)
                    return ("A", tt)

                def buf_of(s, tt):
                    return (t_xa if key_of(s, tt)[0] == "A" else t_xn)[tt]
                g_free = [None, None]
                wbusy = {}
                fuse_ready = {}
                lnyT_ready = {}
                y_sq_free = [0]
                XS = ["xa0", "xa1", "xb0", "xb1"]
                ABS = ["xat0", "xat1"]

                def issue_gu_block(ib):
                    wslot = ib % 2
                    for nm, w8t in (("g", wg8), ("u", wu8)):
                        buf = (t_wg if nm == "g" else t_wu)[wslot]
                        key = (nm, wslot)
                        if key in wbusy:
                            P.wait("gpsimd", wbusy[key][0], wbusy[key][1])
                        sl = SL[("wg" if nm == "g" else "wu") + str(wslot)]
                        P.op("gpsimd", "dma_start",
                             out=buf[:],
                             in_=w8t[:, ib * 4096:(ib + 1) * 4096],
                             inc=sl, inck=16)

                def issue_wd_tile(dc, icg):
                    wdslot = icg % 2
                    key = ("d", wdslot)
                    if key in wbusy:
                        P.wait("gpsimd", wbusy[key][0], wbusy[key][1])
                    P.op("gpsimd", "dma_start",
                         out=t_wd[wdslot][:],
                         in_=wd8[:, (dc * 16 + icg) * 2048:(dc * 16 + icg) * 2048 + 2048],
                         inc=SL[f"wd{wdslot}"], inck=16)

                front_y = {}

                def emit_front_load_ln(s, tt):
                    gt = s * NTT + tt
                    b = buf_of(s, tt)
                    fk = key_of(s, tt)
                    sx = SL[XS[tt]]
                    if bfree[fk] is not None:
                        P.wait("scalar", bfree[fk][0], bfree[fk][1])
                    P.op("scalar", "dma_start", out=b[:],
                         in_=xin[gt * 128:(gt + 1) * 128, :], inc=sx, inck=16)
                    if xadd is not None:
                        sb = SL[ABS[0]]
                        if ab_free[0] is not None:
                            P.wait("scalar", ab_free[0][0], ab_free[0][1])
                        P.op("scalar", "dma_start", out=t_ab[0][:],
                             in_=xadd[gt * 128:(gt + 1) * 128, :], inc=sb, inck=16)
                        P.wait("vector", sx, sx.n)
                        P.wait("vector", sb, sb.n)
                        P.drain("vector")
                        P.op("vector", "tensor_add", b[:], b[:], t_ab[0][:],
                             inc=sDVE)
                        ab_free[0] = (sDVE, sDVE.n)
                        rdy = (sDVE, sDVE.n)
                    else:
                        rdy = (sx, sx.n)
                    st = (tt % 2) * 8
                    P.wait("vector", rdy[0], rdy[1])
                    P.wait("scalar", rdy[0], rdy[1])
                    if y_sq_free[0]:
                        P.wait("scalar", sPE, y_sq_free[0])
                    emit_ln(b[:], t_sq[:], t_sq[:], t_stat, st, 128)
                    front_y[(s, tt)] = sACT.n

                def emit_front_tr(s, tt):
                    P.wait("tensor", sACT, front_y[(s, tt)])
                    transpose_128(
                        t_sq,
                        lambda hcc, tt=tt: t_yT[:, hcc * STRIPE + tt * 128: hcc * STRIPE + tt * 128 + 128],
                        NHC, t_i128b[:], 128)
                    y_sq_free[0] = sPE.n

                for s in range(NS):
                    tts = range(NTT) if (phase != 1 or s == 0) else range(2, NTT)
                    for tt in tts:
                        emit_front_load_ln(s, tt)
                        emit_front_tr(s, tt)
                    lnyT_ready[s] = sDVE.n

                    issue_gu_block(0)
                    for ib in range(INTER // 256):
                        wslot = ib % 2
                        if ib + 1 < INTER // 256:
                            issue_gu_block(ib + 1)
                        for icc in range(2):
                            ic = ib * 2 + icc
                            pg = pA[0 + 2 * (ic % 2)]
                            pu = pA[1 + 2 * (ic % 2)]
                            P.wait("tensor", SL[f"wg{wslot}"], SL[f"wg{wslot}"].n)
                            P.wait("tensor", SL[f"wu{wslot}"], SL[f"wu{wslot}"].n)
                            if ic < 2:
                                P.wait("tensor", sDVE, lnyT_ready[s])
                            else:
                                if ic < 4:
                                    P.wait("tensor", sDVE, lnyT_ready[s])
                                P.wait("tensor", sDVE, fuse_ready[(s, ic - 2)])
                            for hc in range(NHC):
                                P.op("tensor", "matmul", pg[:, 0:512],
                                     t_wg[wslot][:, hc * 256 + icc * 128: hc * 256 + icc * 128 + 128],
                                     t_yT[:, hc * STRIPE:(hc + 1) * STRIPE],
                                     start=(hc == 0), stop=(hc == NHC - 1),
                                     inc=(sPE if hc == NHC - 1 else None))
                                P.op("tensor", "matmul", pu[:, 0:512],
                                     t_wu[wslot][:, hc * 256 + icc * 128: hc * 256 + icc * 128 + 128],
                                     t_yT[:, hc * STRIPE:(hc + 1) * STRIPE],
                                     start=(hc == 0), stop=(hc == NHC - 1),
                                     inc=(sPE if hc == NHC - 1 else None))
                            u_mm = sPE.n
                            g_mm = u_mm - 1
                            if icc == 1:
                                wbusy[("g", wslot)] = (sPE, u_mm)
                                wbusy[("u", wslot)] = (sPE, u_mm)
                            eslot = ic % 2
                            P.wait("scalar", sPE, g_mm)
                            if g_free[eslot] is not None:
                                P.wait("scalar", sDVE, g_free[eslot])
                            P.op("scalar", "activation", t_g[eslot][:], pg[:, 0:512], AF.Silu,
                                 scale=t_sgp[:, ic:ic + 1], inc=sACT)
                            P.wait("vector", sACT, sACT.n)
                            P.wait("vector", sPE, u_mm)
                            P.drain("vector")
                            P.op("vector", "scalar_tensor_tensor", t_fuse[:, ic * 512:(ic + 1) * 512],
                                 pu[:, 0:512], t_sup[:, ic:ic + 1], t_g[eslot][:],
                                 ALU.mult, ALU.mult, inc=sDVE)
                            fuse_ready[(s, ic)] = sDVE.n
                            g_free[eslot] = sDVE.n

                    dpsum = [pB[0], pB[1], pA[0], pA[1]]
                    ev_done = {}
                    issue_wd_tile(0, 0)
                    for dc in range(4):
                        for icg in range(16):
                            wdslot = icg % 2
                            if icg + 1 < 16:
                                issue_wd_tile(dc, icg + 1)
                            elif dc + 1 < 4:
                                issue_wd_tile(dc + 1, 0)
                            P.wait("tensor", SL[f"wd{wdslot}"], SL[f"wd{wdslot}"].n)
                            if icg == 0:
                                P.wait("tensor", sDVE, fuse_ready[(s, 63)])
                                for tt in range(NTT):
                                    if (dc - 1, tt) in ev_done:
                                        P.wait("tensor", sDVE, ev_done[(dc - 1, tt)])
                            for j in range(4):
                                ic = icg * 4 + j
                                for tt in range(NTT):
                                    P.op("tensor", "matmul", dpsum[tt][:, 0:512],
                                         t_fuse[:, ic * 512 + tt * 128: ic * 512 + tt * 128 + 128],
                                         t_wd[wdslot][:, j * 512:(j + 1) * 512],
                                         start=(ic == 0), stop=(ic == 63),
                                         inc=(sPE if (tt == NTT - 1 and j == 3) else None))
                            wbusy[("d", wdslot)] = (sPE, sPE.n)
                        down_mm = sPE.n
                        for tt in range(NTT):
                            P.wait("vector", sPE, down_mm)
                            dslot = (tt % 2) if phase == 2 else 0
                            t_dt = t_ab[0][:, dslot * 512:dslot * 512 + 512]
                            P.drain("vector")
                            P.op("vector", "tensor_mul", t_dt,
                                 dpsum[tt][:, 0:512], t_sdb[:, dc * 512:(dc + 1) * 512])
                            P.drain("vector")
                            b = buf_of(s, tt)
                            P.op("vector", "tensor_add", b[:, dc * 512:(dc + 1) * 512],
                                 b[:, dc * 512:(dc + 1) * 512], t_dt,
                                 inc=sDVE)
                            ev_done[(dc, tt)] = sDVE.n
                        if phase == 1 and s + 1 < NS:
                            if dc == 0:
                                emit_front_load_ln(s + 1, 0)
                            elif dc == 1:
                                emit_front_tr(s + 1, 0)
                                emit_front_load_ln(s + 1, 1)
                            elif dc == 2:
                                emit_front_tr(s + 1, 1)
                    ab_free[0] = (sDVE, sDVE.n)
                    for tt in range(NTT):
                        gt = s * NTT + tt
                        P.wait("scalar", sDVE, ev_done[(3, tt)])
                        sx = SL[XS[tt]]
                        b = buf_of(s, tt)
                        P.op("scalar", "dma_start", out=xout[gt * 128:(gt + 1) * 128, :],
                             in_=b[:], inc=sx, inck=16)
                        bfree[key_of(s, tt)] = (sx, sx.n)

        def barrier():
            P.wait("gpsimd", dQ, CASTS)
            for k in ("wg0", "wg1", "wu0", "wu1", "wd0", "wd1"):
                P.wait("gpsimd", SL[k], SL[k].n)
            for k in ("xa0", "xa1", "xb0", "xb1", "xat0", "xat1", "ao0", "ao1"):
                P.wait("scalar", SL[k], SL[k].n)
            P.wait("sync", dT, dT.n)
            for k in ("wo0", "wo1", "awg0", "awg1", "awu0", "awu1",
                      "awg2", "awg3", "awu2", "awu3"):
                P.wait("sync", SL[k], SL[k].n)
            target = sBAR.n + 5
            for e in Prog.ENGINES:
                if e in ("scalar", "vector", "tensor", "gpsimd"):
                    P.drain(e)
                P.op(e, "nop", inc=sBAR)
            for e in Prog.ENGINES:
                P.wait(e, sBAR, target)

        def attn_phase():
            with ExitStack() as rctx:
                RE = rctx.enter_context
                t_xat = [RE(nc.sbuf_tensor(f"a_x{i}", [64, D], F32)) for i in range(2)]
                t_vh = [RE(nc.sbuf_tensor(f"a_vh{i}", [64, D], BF16)) for i in range(8)]
                t_sq6 = RE(nc.sbuf_tensor("a_sq6", [64, D], BF16))
                t_oT = RE(nc.sbuf_tensor("a_oT", [128, NHC * STRIPE], BF16))
                t_z = RE(nc.sbuf_tensor("a_z", [128, NHC * STRIPE], BF16))

                t_sgt = [RE(nc.sbuf_tensor(f"a_sg{i}", [128, 512], BF16)) for i in range(2)]
                t_aoe = [RE(nc.sbuf_tensor(f"a_ao{i}", [128, 512], F32)) for i in range(2)]
                t_zt = t_aoe[0]
                t_wot = [RE(nc.sbuf_tensor(f"a_wo{i}", [128, 512], BF16)) for i in range(2)]
                t_qz = [RE(nc.sbuf_tensor(f"a_qz{i}", [64, 128], F32)) for i in range(2)]
                t_qb = [RE(nc.sbuf_tensor(f"a_qb{i}", [64, 128], BF16)) for i in range(2)]
                t_kf = [RE(nc.sbuf_tensor(f"a_kf{i}", [64, 128], F32)) for i in range(2)]
                t_kb = [RE(nc.sbuf_tensor(f"a_kb{i}", [64, 128], BF16)) for i in range(2)]
                t_e1 = [RE(nc.sbuf_tensor(f"a_e1{i}", [64, 128], F32)) for i in range(2)]
                t_sp = [RE(nc.sbuf_tensor(f"a_sp{i}", [64, 128], F32)) for i in range(2)]
                t_nb = [RE(nc.sbuf_tensor(f"a_nb{i}", [64, 128], F32)) for i in range(2)]
                t_kzf = [RE(nc.sbuf_tensor(f"a_kzf{i}", [64, 128], F32)) for i in range(2)]
                t_nbT = [RE(nc.sbuf_tensor(f"a_nbT{i}", [128, 64], F32)) for i in range(2)]
                t_eB = [RE(nc.sbuf_tensor(f"a_eB{i}", [128, 64], F32)) for i in range(2)]
                t_emB = [RE(nc.sbuf_tensor(f"a_emB{i}", [128, 64], F32)) for i in range(2)]
                t_ed = [RE(nc.sbuf_tensor(f"a_ed{i}", [128, 64], F32)) for i in range(2)]
                t_eld = [RE(nc.sbuf_tensor(f"a_eld{i}", [128, 1], F32)) for i in range(8)]
                t_qT = [RE(nc.sbuf_tensor(f"a_qT{i}", [128, 64], BF16)) for i in range(2)]
                t_kT = [RE(nc.sbuf_tensor(f"a_kT{i}", [128, 64], BF16)) for i in range(2)]
                t_qeB = [RE(nc.sbuf_tensor(f"a_qeB{i}", [128, 64], BF16)) for i in range(8)]
                t_kme = [RE(nc.sbuf_tensor(f"a_kme{i}", [128, 64], BF16)) for i in range(2)]
                t_kdT = [RE(nc.sbuf_tensor(f"a_kdT{i}", [128, 64], BF16)) for i in range(2)]
                t_kd = [RE(nc.sbuf_tensor(f"a_kd{i}", [64, 128], BF16)) for i in range(8)]
                t_AT = [RE(nc.sbuf_tensor(f"a_AT{i}", [64, 64], BF16)) for i in range(8)]
                t_dg = [RE(nc.sbuf_tensor(f"a_dg{i}", [128, 128], BF16)) for i in range(8)]
                t_A2 = [RE(nc.sbuf_tensor(f"a_A2{i}", [64, 64], BF16)) for i in range(4)]
                t_qe2 = [RE(nc.sbuf_tensor(f"a_qe2{i}", [128, 64], BF16)) for i in range(4)]
                t_sm = RE(nc.sbuf_tensor("a_sm", [64, 24], F32))
                t_yT2 = RE(nc.sbuf_tensor("a_yT2", [128, NHC * STRIPE], BF16))
                t_rr = t_aoe[0]
                t_rsb = t_aoe[1]
                yT_of = lambda s: (t_yT if s % 2 == 0 else t_yT2)

                P.op("vector", "memset", t_Sb[:], 0.0, inc=sDVE)
                x_free = [None, None]
                vh_free = [None] * 8
                slot_guard = [None] * 8
                wt_busy = {}
                st_q = {"q_act": 0, "k_act": 0, "sup": [0, 0, 0, 0], "sb": 0, "oT_ev": 0,
                        "epi_act": 0, "pf": 0}
                pg_guard = {0: {}, 1: {}}
                prep_out = {}

                def load_x_chunk(gch):
                    slot = gch % 2
                    sl = SL[f"xat{slot}"]
                    if x_free[slot] is not None:
                        P.wait("scalar", x_free[slot][0], x_free[slot][1])
                    P.op("scalar", "dma_start", out=t_xat[slot][:],
                         in_=x1s[gch * 64:(gch + 1) * 64, :], inc=sl, inck=16)
                    return (sl, sl.n)

                vh_tr_of = {}

                def emit_prep_stages(gch):
                    xld = load_x_chunk(gch)
                    p = gch % 2
                    s, ch = divmod(gch, NCH)
                    G = pg_guard[p]
                    st = p * 8
                    P.wait("vector", xld[0], xld[1])
                    P.wait("scalar", xld[0], xld[1])
                    if vh_free[ch] is not None:
                        P.wait("scalar", vh_free[ch][0], vh_free[ch][1])
                    if gch > 0:
                        P.wait("scalar", sPE, vh_tr_of[gch - 1])
                    emit_ln(t_xat[p][:], t_vh[ch][:], t_sq6[0:64, :], t_sm, st, 64)
                    vh_rdy = sACT.n
                    x_free[p] = (sACT, vh_rdy)
                    yield
                    P.wait("tensor", sACT, vh_rdy)
                    yTs = yT_of(s)
                    transpose_128(
                        t_vh[ch],
                        lambda hcc, ch=ch, yTs=yTs: yTs[:, hcc * STRIPE + ch * 64: hcc * STRIPE + ch * 64 + 64],
                        NHC, t_i64b[:], 64)
                    vh_tr = sPE.n
                    vh_tr_of[gch] = vh_tr
                    P.wait("tensor", sDVE, max(st_q["q_act"], st_q["k_act"]))
                    for hc in range(NHC):
                        P.op("tensor", "matmul", pA[0][0:64, 0:128],
                             yTs[:, hc * STRIPE + ch * 64: hc * STRIPE + ch * 64 + 64],
                             t_wqT[:, hc * DK:(hc + 1) * DK],
                             start=(hc == 0), stop=(hc == NHC - 1),
                             inc=(sPE if hc == NHC - 1 else None))
                        P.op("tensor", "matmul", pA[1][0:64, 0:128],
                             yTs[:, hc * STRIPE + ch * 64: hc * STRIPE + ch * 64 + 64],
                             t_wkT[:, hc * DK:(hc + 1) * DK],
                             start=(hc == 0), stop=(hc == NHC - 1),
                             inc=(sPE if hc == NHC - 1 else None))
                    kz_mm = sPE.n
                    qz_mm = kz_mm - 1
                    P.wait("vector", sPE, kz_mm)
                    if "qzf_r" in G:
                        P.wait("vector", sACT, G["qzf_r"])
                    P.op("vector", "tensor_copy", t_nb[p][:], pA[0][0:64, 0:128])
                    P.op("vector", "tensor_copy", t_kzf[p][:], pA[1][0:64, 0:128], inc=sDVE)
                    st_q["q_act"] = sDVE.n
                    st_q["k_act"] = sDVE.n
                    qkz_ev = sDVE.n
                    P.wait("scalar", sDVE, qkz_ev)
                    if "qz_w" in G:
                        P.wait("scalar", G["qz_w"][0], G["qz_w"][1])
                    P.op("scalar", "activation", t_qz[p][:], t_nb[p][:], AF.Exp,
                         accum_out=t_sm[:, 17:18], inc=sACT)
                    G["qzf_r"] = sACT.n
                    P.wait("vector", sACT, sACT.n)
                    P.op("vector", "reciprocal", t_sm[:, 18:19], t_sm[:, 17:18])
                    P.drain("vector")
                    if "qb_r" in G:
                        P.wait("vector", sPE, G["qb_r"])
                    P.op("vector", "tensor_scalar_mul", t_qb[p][:], t_qz[p][:], t_sm[:, 18:19],
                         inc=sDVE)
                    q_rdy = (sDVE, sDVE.n)
                    G["qz_w"] = q_rdy
                    if "e1_w" in G:
                        P.wait("scalar", sDVE, G["e1_w"])
                    P.op("scalar", "activation", t_e1[p][:], t_kzf[p][:], AF.Exp,
                         scale=-1.0, inc=sACT)
                    P.wait("vector", sACT, sACT.n)
                    if "sp_r" in G:
                        P.wait("vector", G["sp_r"][0], G["sp_r"][1])
                    P.drain("vector")
                    P.op("vector", "tensor_scalar_add", t_kf[p][:], t_e1[p][:], 1.0)
                    P.drain("vector")
                    P.op("vector", "reciprocal", t_kf[p][:], t_kf[p][:], inc=sDVE)
                    G["e1_w"] = sDVE.n
                    k_rdy = sDVE.n
                    P.wait("gpsimd", sDVE, k_rdy)
                    if "qb_r" in G:
                        P.wait("gpsimd", sPE, G["qb_r"])
                    P.drain("gpsimd")
                    P.op("gpsimd", "tensor_copy", t_kb[p][:], t_kf[p][:], inc=sPL)
                    kb_rdy = (sPL, sPL.n)
                    P.wait("scalar", sDVE, k_rdy)
                    P.op("scalar", "activation", t_e1[p][:], t_kf[p][:], AF.Exp, scale=-1.0)
                    P.drain("scalar")
                    P.op("scalar", "activation", t_e1[p][:], t_e1[p][:], AF.Ln,
                         bias=t_cm[0:64, 1:2], inc=sACT)
                    P.wait("gpsimd", sACT, sACT.n)
                    if "sp_mm" in G:
                        P.wait("gpsimd", sPE, G["sp_mm"])
                    P.drain("gpsimd")
                    P.op("gpsimd", "tensor_add", t_sp[p][:], t_kf[p][:], t_e1[p][:], inc=sPL)
                    sp_rdy = (sPL, sPL.n)
                    G["sp_r"] = sp_rdy
                    yield
                    P.wait("tensor", sp_rdy[0], sp_rdy[1])
                    P.wait("tensor", sDVE, st_q["pf"])
                    P.op("tensor", "matmul", pF[:, 128:192], t_sp[p][:], t_triu[:],
                         start=True, stop=True, inc=sPE)
                    G["sp_mm"] = sPE.n
                    P.wait("vector", sPE, sPE.n)
                    if "ed_r" in G:
                        P.wait("vector", sACT, G["ed_r"])
                    P.op("vector", "tensor_copy", t_nbT[p][:], pF[:, 128:192], inc=sDVE)
                    nbT_rdy = sDVE.n
                    st_q["pf"] = nbT_rdy
                    P.wait("scalar", sDVE, nbT_rdy)
                    if "sc_r" in G:
                        P.wait("scalar", G["sc_r"][0], G["sc_r"][1])
                    P.op("scalar", "activation", t_eB[p][:], t_nbT[p][:], AF.Exp, scale=-1.0)
                    P.drain("scalar")
                    P.op("scalar", "activation", t_emB[p][:], t_nbT[p][:], AF.Exp, inc=sACT)
                    exps_rdy = sACT.n
                    P.wait("vector", sDVE, nbT_rdy)
                    P.drain("vector")
                    P.op("vector", "tensor_scalar", t_ed[p][:], t_nbT[p][:], t_nbT[p][:, 63:64],
                         None, ALU.subtract, inc=sDVE)
                    P.wait("scalar", sDVE, sDVE.n)
                    if slot_guard[ch] is not None:
                        P.wait("scalar", sDVE, slot_guard[ch]["stt"])
                    P.drain("scalar")
                    P.op("scalar", "activation", t_ed[p][:], t_ed[p][:], AF.Exp)
                    P.drain("scalar")
                    P.op("scalar", "activation", t_eld[ch][:], t_nbT[p][:, 63:64], AF.Exp,
                         scale=-1.0, inc=sACT)
                    ed_rdy = sACT.n
                    G["ed_r"] = ed_rdy
                    P.wait("vector", sACT, ed_rdy)
                    if slot_guard[ch] is not None:
                        P.wait("vector", sPE, slot_guard[ch]["mm"])
                    P.drain("vector")
                    P.op("vector", "tensor_scalar_mul", t_dg[ch][:], t_i128b[:],
                         t_eld[ch][:, 0:1], inc=sDVE)
                    dg_rdy = sDVE.n
                    P.wait("tensor", q_rdy[0], q_rdy[1])
                    P.wait("tensor", kb_rdy[0], kb_rdy[1])
                    P.wait("tensor", sDVE, ptfree[0])
                    P.op("tensor", "transpose", pT[:, 0:64], t_qb[p][:], t_i64b[:])
                    P.op("tensor", "transpose", pT[:, 64:128], t_kb[p][:], t_i64b[:], inc=sPE)
                    G["qb_r"] = sPE.n
                    P.wait("vector", sPE, sPE.n)
                    if "qT_r" in G:
                        P.wait("vector", G["qT_r"][0], G["qT_r"][1])
                    P.op("vector", "tensor_copy", t_qT[p][:], pT[:, 0:64])
                    P.op("vector", "tensor_copy", t_kT[p][:], pT[:, 64:128], inc=sDVE)
                    qkT_rdy = sDVE.n
                    ptfree[0] = qkT_rdy
                    P.wait("tensor", sDVE, qkT_rdy)
                    P.wait("gpsimd", sDVE, qkT_rdy)
                    P.wait("gpsimd", sACT, exps_rdy)
                    if slot_guard[ch] is not None:
                        P.wait("gpsimd", sPE, slot_guard[ch]["mm"])
                    P.drain("gpsimd")
                    P.op("gpsimd", "tensor_mul", t_qeB[ch][:], t_qT[p][:], t_eB[p][:])
                    P.drain("gpsimd")
                    P.op("gpsimd", "tensor_mul", t_kme[p][:], t_kT[p][:], t_emB[p][:], inc=sPL)
                    P.wait("gpsimd", sACT, ed_rdy)
                    P.wait("gpsimd", sPE, st_q.get("a2_mm", 0))
                    P.drain("gpsimd")
                    P.op("gpsimd", "tensor_mul", t_kdT[p][:], t_kT[p][:], t_ed[p][:], inc=sPL)
                    scaled_rdy = (sPL, sPL.n)
                    G["sc_r"] = scaled_rdy
                    G["qT_r"] = scaled_rdy
                    yield
                    P.wait("tensor", scaled_rdy[0], scaled_rdy[1])
                    P.wait("tensor", sDVE, st_q["pf"])
                    P.op("tensor", "matmul", pF[0:64, 192:256], t_kme[p][:], t_qeB[ch][:],
                         start=True, stop=True, inc=sPE)
                    at_mm = sPE.n
                    P.op("tensor", "transpose", pT[0:64, 0:128], t_kdT[p][:], t_i128b[:], inc=sPE)
                    kd_mm = sPE.n
                    P.wait("vector", sPE, at_mm)
                    if slot_guard[ch] is not None:
                        P.wait("vector", sPE, slot_guard[ch]["mm"])
                    P.op("vector", "tensor_mul", t_AT[ch][:], pF[0:64, 192:256], t_triu[:],
                         inc=sDVE)
                    at_rdy = sDVE.n
                    st_q["pf"] = at_rdy
                    P.wait("vector", sPE, kd_mm)
                    if slot_guard[ch] is not None:
                        P.wait("vector", sPE, slot_guard[ch]["mm"])
                    P.op("vector", "tensor_copy", t_kd[ch][:], pT[0:64, 0:128], inc=sDVE)
                    kd_rdy = sDVE.n
                    ptfree[0] = kd_rdy
                    a2_rdy = 0
                    if ch % 2 == 1:
                        p2 = (ch - 1) % 2
                        P.wait("tensor", sDVE, st_q["pf"])
                        P.op("tensor", "matmul", pF[0:64, 256:320], t_kdT[p2][:], t_qeB[ch][:],
                             start=True, stop=True, inc=sPE)
                        a2_mm = sPE.n
                        st_q["a2_mm"] = a2_mm
                        P.wait("vector", sPE, a2_mm)
                        if slot_guard[ch] is not None:
                            P.wait("vector", sPE, slot_guard[ch]["mm"])
                        P.op("vector", "tensor_copy", t_A2[ch // 2][:], pF[0:64, 256:320], inc=sDVE)
                        st_q["pf"] = sDVE.n
                        P.drain("vector")
                        P.op("vector", "tensor_scalar_mul", t_qe2[ch // 2][:], t_qeB[ch][:],
                             t_eld[ch - 1][:, 0:1], inc=sDVE)
                        a2_rdy = sDVE.n
                    prep_out[gch] = {"vh_tr": vh_tr, "at_rdy": at_rdy, "kd_rdy": kd_rdy,
                                     "ed_rdy": ed_rdy, "a2_rdy": a2_rdy, "dg_rdy": dg_rdy}

                def emit_body_pair(gch):
                    s, ch0 = divmod(gch, NCH)
                    ch1 = ch0 + 1
                    po0 = prep_out[gch]
                    po1 = prep_out[gch + 1]
                    P.wait("tensor", sDVE, po0["kd_rdy"])
                    P.wait("tensor", sDVE, po0["at_rdy"])
                    P.wait("tensor", sDVE, po1["kd_rdy"])
                    P.wait("tensor", sDVE, po1["at_rdy"])
                    P.wait("tensor", sDVE, po1["a2_rdy"])
                    if st_q["sb"]:
                        P.wait("tensor", sDVE, st_q["sb"])
                    if st_q["oT_ev"]:
                        P.wait("tensor", sDVE, st_q["oT_ev"])
                    for m8 in range(8):
                        for half in range(2):
                            m = m8 + 8 * half
                            P.op("tensor", "matmul", pB[half][:, m8 * 64:m8 * 64 + 64],
                                 t_vh[ch0][:, m * 128:(m + 1) * 128], t_AT[ch0][:],
                                 start=True, stop=False)
                        for half in range(2):
                            m = m8 + 8 * half
                            P.op("tensor", "matmul", pB[half][:, m8 * 64:m8 * 64 + 64],
                                 t_Sb[:, m * 128:(m + 1) * 128], t_qeB[ch0][:],
                                 start=False, stop=True,
                                 inc=(sPE if m8 == 7 else None))
                    oT0 = [sPE.n - 1, sPE.n]
                    for m8 in range(8):
                        for half in range(2):
                            m = m8 + 8 * half
                            P.op("tensor", "matmul", pA[2 + half][:, m8 * 64:m8 * 64 + 64],
                                 t_vh[ch1][:, m * 128:(m + 1) * 128], t_AT[ch1][:],
                                 start=True, stop=False)
                        for half in range(2):
                            m = m8 + 8 * half
                            P.op("tensor", "matmul", pA[2 + half][:, m8 * 64:m8 * 64 + 64],
                                 t_vh[ch0][:, m * 128:(m + 1) * 128], t_A2[ch1 // 2][:],
                                 start=False, stop=False)
                        for half in range(2):
                            m = m8 + 8 * half
                            P.op("tensor", "matmul", pA[2 + half][:, m8 * 64:m8 * 64 + 64],
                                 t_Sb[:, m * 128:(m + 1) * 128], t_qe2[ch1 // 2][:],
                                 start=False, stop=True,
                                 inc=(sPE if m8 == 7 else None))
                    oT1 = [sPE.n - 1, sPE.n]
                    if ch0 == 0 and st_q["epi_act"]:
                        P.wait("vector", sACT, st_q["epi_act"])
                    for half in range(2):
                        P.wait("vector", sPE, oT0[half])
                        P.op("vector", "tensor_copy",
                             t_oT[:].rearrange("p (m t) -> p m t", m=NHC)[:, half * 8:half * 8 + 8,
                                                                          ch0 * 64:(ch0 + 1) * 64],
                             pB[half][:].rearrange("p (m t) -> p m t", m=8), inc=sDVE)
                    ev0 = sDVE.n
                    for half in range(2):
                        P.wait("vector", sPE, oT1[half])
                        P.op("vector", "tensor_copy",
                             t_oT[:].rearrange("p (m t) -> p m t", m=NHC)[:, half * 8:half * 8 + 8,
                                                                          ch1 * 64:(ch1 + 1) * 64],
                             pA[2 + half][:].rearrange("p (m t) -> p m t", m=8), inc=sDVE)
                    ev1 = sDVE.n
                    st_q["oT_ev"] = ev1
                    kv0_banks = [pA[0], pA[1], pF, pB[0]]
                    kv1_banks = [pB[1], pA[2], pA[3], pA[0]]
                    P.wait("tensor", sDVE, st_q["q_act"])
                    P.wait("tensor", sDVE, st_q["pf"])
                    P.wait("tensor", sDVE, ev0)
                    P.wait("tensor", sDVE, po0["dg_rdy"])
                    for n in range(4):
                        P.op("tensor", "matmul", kv0_banks[n][:, 0:512], t_dg[ch0][:],
                             t_Sb[:, n * 512:(n + 1) * 512], start=True, stop=False)
                    for n in range(4):
                        P.op("tensor", "matmul", kv0_banks[n][:, 0:512], t_kd[ch0][:],
                             t_vh[ch0][:, n * 512:(n + 1) * 512], start=False, stop=True,
                             inc=(sPE if n == 3 else None))
                    kv0 = sPE.n
                    P.wait("vector", sPE, kv0)
                    P.drain("vector")
                    s_up0 = []
                    for n in range(4):
                        P.op("vector", "tensor_copy", t_Sb[:, n * 512:(n + 1) * 512],
                             kv0_banks[n][:, 0:512], inc=sDVE)
                        s_up0.append(sDVE.n)
                    P.wait("tensor", sDVE, s_up0[3])
                    P.wait("tensor", sDVE, ev1)
                    P.wait("tensor", sDVE, po1["dg_rdy"])
                    for n in range(4):
                        P.op("tensor", "matmul", kv1_banks[n][:, 0:512], t_dg[ch1][:],
                             t_Sb[:, n * 512:(n + 1) * 512], start=True, stop=False)
                    for n in range(4):
                        P.op("tensor", "matmul", kv1_banks[n][:, 0:512], t_kd[ch1][:],
                             t_vh[ch1][:, n * 512:(n + 1) * 512], start=False, stop=True,
                             inc=(sPE if n == 3 else None))
                    kv1b = sPE.n
                    P.wait("vector", sPE, kv1b)
                    P.drain("vector")
                    s_up1 = []
                    for n in range(4):
                        P.op("vector", "tensor_copy", t_Sb[:, n * 512:(n + 1) * 512],
                             kv1_banks[n][:, 0:512], inc=sDVE)
                        s_up1.append(sDVE.n)
                    st_q["sup"] = [s_up1[3]] * 4
                    st_q["sb"] = s_up1[3]
                    st_q["pf"] = s_up1[3]
                    st_q["q_act"] = s_up1[3]
                    st_q["k_act"] = s_up1[3]
                    vh_free[ch0] = (sDVE, s_up1[3])
                    vh_free[ch1] = (sDVE, s_up1[3])
                    slot_guard[ch0] = {"mm": kv1b, "stt": s_up1[3]}
                    slot_guard[ch1] = {"mm": kv1b, "stt": s_up1[3]}

                epi = {}

                def emit_rms(e):
                    sq_prev = 0
                    P.wait("vector", sDVE, st_q["oT_ev"])
                    P.wait("gpsimd", sDVE, st_q["oT_ev"])
                    P.wait("tensor", sDVE, sDVE.n)
                    mm_done = {}
                    for mc in range(NHC):
                        eng, sem = (("vector", sDVE) if mc % 2 == 0 else ("gpsimd", sPL))
                        if mc >= 2:
                            P.wait(eng, sPE, mm_done[mc - 2])
                        P.drain(eng)
                        P.op(eng, "tensor_mul", t_sgt[mc % 2][:],
                             t_oT[:, mc * STRIPE:(mc + 1) * STRIPE],
                             t_oT[:, mc * STRIPE:(mc + 1) * STRIPE], inc=sem)
                        P.wait("tensor", sem, sem.n)
                        if mc == 0:
                            P.wait("tensor", sDVE, st_q["sup"][3])
                        P.op("tensor", "matmul", pA[2][0:1, 0:512],
                             t_ones[:, 0:1], t_sgt[mc % 2][:],
                             start=(mc == 0), stop=(mc == NHC - 1), inc=sPE)
                        mm_done[mc] = sPE.n
                    rms_mm = sPE.n
                    P.wait("vector", sPE, rms_mm)
                    P.wait("vector", SL["ao0"], SL["ao0"].n)
                    P.op("vector", "tensor_scalar_mul", t_rr[0:1, 0:512], pA[2][0:1, 0:512], 1.0 / D,
                         inc=sDVE)
                    rr_ev = sDVE.n
                    P.wait("scalar", sDVE, rr_ev)
                    P.op("scalar", "activation", t_rr[0:1, 0:512], t_rr[0:1, 0:512], AF.Ln,
                         bias=t_cm[0:1, 0:1])
                    P.drain("scalar")
                    P.op("scalar", "activation", t_rr[0:1, 0:512], t_rr[0:1, 0:512], AF.Exp,
                         scale=-0.5, inc=sACT)
                    P.wait("vector", sACT, sACT.n)
                    P.drain("vector")
                    P.op("vector", "tensor_copy", t_sgt[0][0:1, 0:512], t_rr[0:1, 0:512], inc=sDVE)
                    P.wait("tensor", sDVE, sDVE.n)
                    P.op("tensor", "matmul", pA[2][:, 0:512], t_ones[0:1, :], t_sgt[0][0:1, 0:512],
                         start=True, stop=True, inc=sPE)
                    P.wait("vector", sPE, sPE.n)
                    P.wait("vector", SL["ao1"], SL["ao1"].n)
                    P.op("vector", "tensor_copy", t_rsb[:], pA[2][:, 0:512], inc=sDVE)
                    epi["rsb"] = sDVE.n
                    epi["go_busy"] = {}
                    epi["z_done"] = {}

                def issue_go_pair(mcp):
                    half = mcp % 2
                    for wname, wsrc in (("g", wogT), ("u", wigT)):
                        for wslot in (0, 1):
                            mc = 2 * mcp + wslot
                            key = (wname, wslot, half)
                            if key in wt_busy:
                                P.wait("sync", wt_busy[key][0], wt_busy[key][1])
                            buf = (t_wg if wname == "g" else t_wu)[wslot]
                            sl = SL[("awg" if wname == "g" else "awu") + str(wslot + 2 * half)]
                            P.op("sync", "dma_start",
                                 out=buf[:, half * 2048:half * 2048 + 2048],
                                 in_=wsrc[:, mc * 2048:(mc + 1) * 2048],
                                 inc=sl, inck=16)

                def emit_goT_pair(e, mcp):
                    go_busy = epi["go_busy"]
                    z_done = epi["z_done"]
                    half = mcp % 2
                    co = half * 2048
                    mcs = (2 * mcp, 2 * mcp + 1)
                    if mcp == 0:
                        issue_go_pair(0)
                    if mcp + 1 < NHC // 2:
                        issue_go_pair(mcp + 1)
                    for wslot in (0, 1):
                        for nmw in ("awg", "awu"):
                            sl = SL[nmw + str(wslot + 2 * half)]
                            P.wait("tensor", sl, sl.n)
                    gbank = (pA[2], pA[3]) if mcp % 2 == 0 else (pB[0], pB[1])
                    if mcp == 0:
                        P.wait("tensor", sDVE, epi["rsb"])
                        P.wait("tensor", sDVE, st_q["sup"][3])
                    elif mcp == 1:
                        P.wait("tensor", sDVE, st_q["sup"][3])
                    else:
                        P.wait("tensor", sACT, go_busy[2 * mcp - 4])
                        P.wait("tensor", sACT, go_busy[2 * mcp - 3])
                    for hc in range(NHC):
                        for wslot in (0, 1):
                            P.op("tensor", "matmul", gbank[wslot][:, 0:512],
                                 t_wg[wslot][:, co + hc * 128: co + (hc + 1) * 128],
                                 t_oT[:, hc * STRIPE:(hc + 1) * STRIPE],
                                 start=(hc == 0), stop=False)
                    for hc in range(NHC):
                        for wslot in (0, 1):
                            P.op("tensor", "matmul", gbank[wslot][:, 0:512],
                                 t_wu[wslot][:, co + hc * 128: co + (hc + 1) * 128],
                                 yT_of(e)[:, hc * STRIPE:(hc + 1) * STRIPE],
                                 start=False, stop=(hc == NHC - 1),
                                 inc=(sPE if hc == NHC - 1 else None))
                    go_mm = sPE.n
                    for wname in ("g", "u"):
                        for wslot in (0, 1):
                            wt_busy[(wname, wslot, half)] = (sPE, go_mm)
                    for wslot in (0, 1):
                        mc = mcs[wslot]
                        P.wait("scalar", sPE, go_mm)
                        if mc >= 2:
                            P.wait("scalar", sDVE, z_done[mc - 2])
                        P.op("scalar", "activation", t_sgt[wslot][:], gbank[wslot][:, 0:512],
                             AF.Silu, inc=sACT)
                        go_busy[mc] = sACT.n
                        P.wait("vector", sACT, go_busy[mc])
                        P.wait("vector", sDVE, epi["rsb"])
                        P.wait("vector", SL["ao0"], SL["ao0"].n)
                        P.drain("vector")
                        P.op("vector", "scalar_tensor_tensor", t_zt[:],
                             t_oT[:, mc * STRIPE:(mc + 1) * STRIPE], t_gnw[:, mc:mc + 1], t_rsb[:],
                             ALU.mult, ALU.mult)
                        P.drain("vector")
                        P.op("vector", "tensor_mul", t_z[:, mc * STRIPE:(mc + 1) * STRIPE],
                             t_zt[:], t_sgt[wslot][:], inc=sDVE)
                        z_done[mc] = sDVE.n
                    if mcp == NHC // 2 - 1:
                        epi["zT_all"] = sDVE.n
                        st_q["epi_act"] = sACT.n

                def emit_ao(e, prep_cb=None):
                    go_busy = epi["go_busy"]
                    P.wait("tensor", sDVE, epi["zT_all"])
                    ao_psums = [pB[0], pB[1], pA[2], pA[3]]
                    ao_ev_done = 0
                    for mn in range(4):
                        if mn == 0:
                            P.wait("tensor", sDVE, max(st_q["oT_ev"], st_q["sup"][3]))
                            P.wait("tensor", sACT, go_busy[NHC - 1])
                        else:
                            P.wait("tensor", sDVE, ao_ev_done)
                        for dvc in range(NHC):
                            wslot2 = dvc % 2
                            key = ("wo", wslot2)
                            if key in wt_busy:
                                P.wait("sync", wt_busy[key][0], wt_busy[key][1])
                            sl = SL[f"wo{wslot2}"]
                            P.op("sync", "dma_start", out=t_wot[wslot2][:, 0:512],
                                 in_=woT[dvc * 128:(dvc + 1) * 128, mn * 512:(mn + 1) * 512],
                                 inc=sl, inck=16)
                            P.wait("tensor", sl, sl.n)
                            for tt in range(NTT):
                                P.op("tensor", "matmul", ao_psums[tt][:, 0:512],
                                     t_z[:, dvc * STRIPE + tt * 128: dvc * STRIPE + tt * 128 + 128],
                                     t_wot[wslot2][:, 0:512],
                                     start=(dvc == 0), stop=(dvc == NHC - 1),
                                     inc=(sPE if tt == NTT - 1 else None))
                            wt_busy[key] = (sPE, sPE.n)
                        ao_mm = sPE.n
                        for tt in range(NTT):
                            aslot = tt % 2
                            sl = SL[f"ao{aslot}"]
                            P.wait("vector", sl, sl.n)
                            P.wait("vector", sPE, ao_mm)
                            P.drain("vector")
                            P.op("vector", "tensor_copy", t_aoe[aslot][:], ao_psums[tt][:, 0:512],
                                 inc=sDVE)
                            P.wait("scalar", sDVE, sDVE.n)
                            P.op("scalar", "dma_start",
                                 out=aos[(e * NTT + tt) * 128:(e * NTT + tt + 1) * 128,
                                         mn * 512:(mn + 1) * 512],
                                 in_=t_aoe[aslot][:], inc=sl, inck=16)
                        ao_ev_done = sDVE.n
                        if prep_cb is not None:
                            prep_cb(mn)
                    st_q["sup"] = [ao_ev_done] * 4
                    st_q["oT_ev"] = ao_ev_done

                def pump(gens):
                    alive = []
                    for g in gens:
                        try:
                            next(g)
                            alive.append(g)
                        except StopIteration:
                            pass
                    gens[:] = alive

                gens = []
                for ch in range(NCH):
                    pump(gens)
                    g = emit_prep_stages(ch)
                    next(g)
                    gens.append(g)
                while gens:
                    pump(gens)
                for ch in range(0, NCH, 2):
                    emit_body_pair(ch)
                for s in range(1, NS):
                    emit_rms(s - 1)
                    gens = []
                    nxt = [0]

                    def step(s=s, gens=gens, nxt=nxt):
                        pump(gens)
                        if nxt[0] < NCH:
                            g = emit_prep_stages(s * NCH + nxt[0])
                            nxt[0] += 1
                            next(g)
                            gens.append(g)
                    step()
                    for k in range(NHC // 2):
                        emit_goT_pair(s - 1, k)
                        step()
                    emit_ao(s - 1, prep_cb=lambda mn, f=step: f())
                    while gens:
                        pump(gens)
                    for ch in range(0, NCH, 2):
                        emit_body_pair(s * NCH + ch)
                emit_rms(NS - 1)
                for k in range(NHC // 2):
                    emit_goT_pair(NS - 1, k)
                emit_ao(NS - 1)

        if 1 in phases:
            ffn_phase(1)
        if 2 in phases:
            barrier()
            attn_phase()
        if 3 in phases:
            barrier()
            ffn_phase(2)
        for k in ("xa0", "xa1", "xb0", "xb1"):
            P.wait("scalar", SL[k], SL[k].n)

        block = E(nc.Block())

        @block.sync
        def _(e):
            P.replay(e, "sync")

        @block.scalar
        def _(e):
            P.replay(e, "scalar")

        @block.vector
        def _(e):
            P.replay(e, "vector")

        @block.tensor
        def _(e):
            P.replay(e, "tensor")

        @block.gpsimd
        def _(e):
            P.replay(e, "gpsimd")

    return nc


def _block_gu(w):
    return np.ascontiguousarray(
        w.reshape(NHC, 128, INTER // 256, 256).transpose(1, 2, 0, 3).reshape(128, -1))


def _block_wd(w):
    return np.ascontiguousarray(
        w.reshape(INTER // 512, 4, 128, 4, 512).transpose(2, 3, 0, 1, 4).reshape(128, -1))


def make_consts():
    return dict(
        ci64b=np.eye(64, dtype=np.float32).astype(bf16),
        ci128b=np.eye(128, dtype=np.float32).astype(bf16),
        ci64f=np.eye(64, dtype=np.float32),
        ctriu=np.triu(np.ones((64, 64), np.float32)),
        cones=np.ones((128, 128), np.float32).astype(bf16),
        cmisc=np.tile(np.array([[EPS, 1.0]], np.float32), (128, 1)),
    )


def kernel(L0_attn_hidden_states, L0_attn_Wq, L0_attn_Wk, L0_attn_Wog, L0_attn_Wig,
           L0_attn_Wo, L0_attn_g_norm_w, L0_ffn_w_gate, L0_ffn_w_up, L0_ffn_w_down,
           L0_ffn_s_gate, L0_ffn_s_up, L0_ffn_s_down):
    nc = build_program()
    shared = dict(
        wq=np.ascontiguousarray(L0_attn_Wq), wk=np.ascontiguousarray(L0_attn_Wk),
        wog=np.ascontiguousarray(L0_attn_Wog), wig=np.ascontiguousarray(L0_attn_Wig),
        wo=np.ascontiguousarray(L0_attn_Wo), gnw=np.ascontiguousarray(L0_attn_g_norm_w),
        wg8b=_block_gu(L0_ffn_w_gate), wu8b=_block_gu(L0_ffn_w_up),
        wd8b=_block_wd(L0_ffn_w_down),
        sg=np.ascontiguousarray(L0_ffn_s_gate), su=np.ascontiguousarray(L0_ffn_s_up),
        sd=np.ascontiguousarray(L0_ffn_s_down), **make_consts(),
    )
    in_maps = [dict(x0=np.ascontiguousarray(L0_attn_hidden_states[c]), **shared)
               for c in range(8)]
    res = run_bass_kernel_spmd(nc, in_maps, core_ids=list(range(8)))
    return np.stack([res.results[c]["out"] for c in range(8)], axis=0)
